# version 08
# baseline (speedup 1.0000x reference)
import contextlib
import numpy as np
import ml_dtypes

import concourse.bass as bass
import concourse.mybir as mybir
from concourse.bass_utils import run_bass_kernel_spmd
from concourse.alu_op_type import AluOpType as OP

F32 = mybir.dt.float32
BF16 = mybir.dt.bfloat16
AF = mybir.ActivationFunctionType

NCORES = 8
T = 1024
THF = 512
D = 2048
G = 16
H = 16
DK = 128
CH = 64
HG = 64
U = 64
EPS = 1e-5
QSCALE = float(DK) ** -0.5
FSCALE = float(1.0 / np.sqrt(8192.0))


def _hadamard(n):
    Hm = np.array([[1.0]])
    while Hm.shape[0] < n:
        Hm = np.block([[Hm, Hm], [Hm, -Hm]])
    return Hm


class Sem:
    def __init__(self, es, nc, name):
        self.h = es.enter_context(nc.semaphore(name))
        self.n = 0

    def inc(self, instr, k=1):
        instr.then_inc(self.h, k)
        self.n += k
        return self.n


class DStream:
    def __init__(self, es, nc, name, nslots):
        self.s = [Sem(es, nc, f"{name}_{i}") for i in range(nslots)]
        self.k = nslots
        self.i = 0

    def put(self, instr, inc=16):
        s = self.s[self.i % self.k]
        s.inc(instr, inc)
        self.i += 1
        return (s, s.n)

    def put2(self, instr_a, instr_b):
        s = self.s[self.i % self.k]
        s.inc(instr_a, 16)
        s.inc(instr_b, 16)
        self.i += 1
        return (s, s.n)

    def totals(self):
        return [(s, s.n) for s in self.s]


def wtok(q, tok):
    if isinstance(tok, list):
        for t in tok:
            q.wait_ge(t[0].h, t[1])
    else:
        q.wait_ge(tok[0].h, tok[1])


def build(taps=(), stop_after=None):
    nc = bass.Bass("TRN2", target_bir_lowering=False, debug=False)
    dt = nc.dram_tensor

    x_st = dt("x_st", [128, G * T], F32, kind="ExternalInput").ap()
    wq_st = dt("wq_st", [G, 128, D], F32, kind="ExternalInput").ap()
    wk_st = dt("wk_st", [G, 128, D], F32, kind="ExternalInput").ap()
    wv_st = dt("wv_st", [4, 128, G * THF], F32, kind="ExternalInput").ap()
    wo_st = dt("wo_st", [2, G, 128, 8 * 128], F32, kind="ExternalInput").ap()
    w1_st = dt("w1_st", [HG, 128, D], F32, kind="ExternalInput").ap()
    w3_st = dt("w3_st", [HG, 128, D], F32, kind="ExternalInput").ap()
    w2_st = dt("w2_st", [U, 128, D], F32, kind="ExternalInput").ap()
    fq1T = dt("fq1T", [128, 128], F32, kind="ExternalInput").ap()
    fq2T = dt("fq2T", [128, 128], F32, kind="ExternalInput").ap()
    fk1T = dt("fk1T", [128, 128], F32, kind="ExternalInput").ap()
    fk2T = dt("fk2T", [128, 128], F32, kind="ExternalInput").ap()
    fq1b = dt("fq1b", [128, 1], F32, kind="ExternalInput").ap()
    fq2b = dt("fq2b", [128, 1], F32, kind="ExternalInput").ap()
    fk1b = dt("fk1b", [128, 1], F32, kind="ExternalInput").ap()
    fk2b = dt("fk2b", [128, 1], F32, kind="ExternalInput").ap()
    normw = dt("normw", [128, 1], F32, kind="ExternalInput").ap()
    wsel = dt("wsel", [128, NCORES], F32, kind="ExternalInput").ap()
    identc = dt("identc", [128, 128], BF16, kind="ExternalInput").ap()
    h128c = dt("h128c", [128, 128], BF16, kind="ExternalInput").ap()
    hbdc = dt("hbdc", [128, 128], BF16, kind="ExternalInput").ap()
    maskc = dt("maskc", [64, 256], F32, kind="ExternalInput").ap()
    onesbc = dt("onesbc", [128, 1], BF16, kind="ExternalInput").ap()
    onespc = dt("onespc", [1, 128], F32, kind="ExternalInput").ap()
    onesnc = dt("onesnc", [1, 128], F32, kind="ExternalInput").ap()

    out_st = dt("out_st", [128, G * T], F32, kind="ExternalOutput").ap()

    hid1d = dt("hid1d", [HG * 128, T], BF16).ap()
    attnd = [dt(f"attnd{p}", [D, T], F32).ap() for p in range(2)]
    cc_in = dt("cc_in", [D, 128], F32).ap()
    cc_out = dt("cc_out", [NCORES * D, 128], F32, addr_space="Shared").ap()

    tap_outs = {}

    def mktap(name, width, dtype, rows=128):
        tap_outs[name] = dt("tap_" + name, [rows, width], dtype,
                            kind="ExternalOutput").ap()
        return tap_outs[name]

    es = contextlib.ExitStack()
    with es:
        sLD = Sem(es, nc, "sLD")
        sW = Sem(es, nc, "sW")
        wkS = DStream(es, nc, "wkS", 3)
        wvS = DStream(es, nc, "wvS", 4)
        wqS = DStream(es, nc, "wqS", 2)
        woS = DStream(es, nc, "woS", 4)
        w13S = DStream(es, nc, "w13S", 2)
        w2S = DStream(es, nc, "w2S", 4)
        hpS = DStream(es, nc, "hpS", 4)
        gS = DStream(es, nc, "gS", 2)
        parkS = DStream(es, nc, "parkS", 2)
        h1S = DStream(es, nc, "h1S", 2)
        outldS = DStream(es, nc, "outldS", 2)
        outstS = DStream(es, nc, "outstS", 2)
        tapS = DStream(es, nc, "tapS", 10)
        sPE = Sem(es, nc, "sPE")
        sDV = Sem(es, nc, "sDV")
        sAC = Sem(es, nc, "sAC")
        sCC = Sem(es, nc, "sCC")
        sSTO = Sem(es, nc, "sSTO")
        block = es.enter_context(nc.Block())

        sb = nc.sbuf_tensor
        xn = es.enter_context(sb("xn", [128, G * T], BF16))
        ident_s = es.enter_context(sb("ident_s", [128, 128], BF16))
        h128_s = es.enter_context(sb("h128_s", [128, 128], BF16))
        hbd_s = es.enter_context(sb("hbd_s", [128, 128], BF16))
        mask_s = es.enter_context(sb("mask_s", [64, 256], F32))
        onesb_s = es.enter_context(sb("onesb_s", [128, 1], BF16))
        onesp_s = es.enter_context(sb("onesp_s", [1, 128], F32))
        onesn_s = es.enter_context(sb("onesn_s", [1, 128], F32))
        fq1T_s = es.enter_context(sb("fq1T_s", [128, 128], BF16))
        fq2T_s = es.enter_context(sb("fq2T_s", [128, 128], BF16))
        fk1T_s = es.enter_context(sb("fk1T_s", [128, 128], BF16))
        fk2T_s = es.enter_context(sb("fk2T_s", [128, 128], BF16))
        fq1b_s = es.enter_context(sb("fq1b_s", [128, 1], F32))
        fq2b_s = es.enter_context(sb("fq2b_s", [128, 1], F32))
        fk1b_s = es.enter_context(sb("fk1b_s", [128, 1], F32))
        fk2b_s = es.enter_context(sb("fk2b_s", [128, 1], F32))
        normw_s = es.enter_context(sb("normw_s", [128, 1], F32))
        wsel_s = es.enter_context(sb("wsel_s", [128, NCORES], F32))

        @block.sync
        def _(q):
            for dst, src in [(ident_s, identc), (h128_s, h128c), (hbd_s, hbdc),
                             (mask_s, maskc), (onesb_s, onesbc), (onesp_s, onespc),
                             (onesn_s, onesnc), (fq1b_s, fq1b), (fq2b_s, fq2b),
                             (fk1b_s, fk1b), (fk2b_s, fk2b), (normw_s, normw),
                             (wsel_s, wsel)]:
                sLD.inc(q.dma_start(out=dst[:], in_=src), 16)

        @block.gpsimd
        def _(q):
            for dst, src in [(fq1T_s, fq1T), (fq2T_s, fq2T),
                             (fk1T_s, fk1T), (fk2T_s, fk2T)]:
                sW.inc(q.dma_start(out=dst[:], in_=src), 16)
        n_const = sLD.n
        n_fmapw = sW.n

        def finish_early():
            @block.sync
            def _(q):
                sSTO.inc(q.dma_start(out=out_st, in_=x_st), 16)
                q.wait_ge(sSTO.h, sSTO.n)
                for strm in (parkS, h1S, outstS, tapS):
                    wtok(q, strm.totals())

        es_ln = contextlib.ExitStack()
        x_s = es_ln.enter_context(sb("x_s", [128, G * T], F32))
        xb_r = [es_ln.enter_context(sb(f"xb{i}", [128, T], BF16)) for i in range(2)]
        sq_r = [es_ln.enter_context(sb(f"sq{i}", [128, T], BF16)) for i in range(2)]
        st1 = es_ln.enter_context(sb("st1", [1, 14 * THF], F32))
        xtmp = [es_ln.enter_context(sb(f"xtmp{i}", [128, THF], F32)) for i in range(4)]
        ps_ln = contextlib.ExitStack()
        ps_s1 = [ps_ln.enter_context(nc.psum_tensor(f"ps_s1{i}", [1, THF], F32)) for i in range(2)]
        ps_s2 = [ps_ln.enter_context(nc.psum_tensor(f"ps_s2{i}", [1, THF], F32)) for i in range(2)]
        ps_A = [ps_ln.enter_context(nc.psum_tensor(f"ps_A{i}", [128, THF], F32)) for i in range(2)]
        ps_B = [ps_ln.enter_context(nc.psum_tensor(f"ps_B{i}", [128, THF], F32)) for i in range(2)]

        @block.sync
        def _(q):
            sLD.inc(q.dma_start(out=x_s[:], in_=x_st), 16)
        n_xload = sLD.n

        ln_evt = {}
        for g in range(G):
            @block.vector
            def _(q, g=g):
                if g == 0:
                    q.wait_ge(sLD.h, n_xload)
                if g >= 2:
                    q.wait_ge(sPE.h, ln_evt[(g - 2, "mm")])
                ins = q.tensor_copy(out=xb_r[g % 2][:], in_=x_s[:, g * T:(g + 1) * T])
                sDV.inc(ins)
                ln_evt[(g, "cast")] = sDV.n

            @block.scalar
            def _(q, g=g):
                if g == 0:
                    q.wait_ge(sLD.h, n_xload)
                if g >= 2:
                    q.wait_ge(sPE.h, ln_evt[(g - 2, "mm")])
                ins = q.activation(out=sq_r[g % 2][:], in_=x_s[:, g * T:(g + 1) * T],
                                   func=AF.Square)
                sAC.inc(ins)
                ln_evt[(g, "sq")] = sAC.n

            @block.tensor
            def _(q, g=g):
                q.wait_ge(sDV.h, ln_evt[(g, "cast")])
                q.wait_ge(sAC.h, ln_evt[(g, "sq")])
                for th in range(2):
                    q.matmul(ps_s1[th][:], onesb_s[:], xb_r[g % 2][:, th * THF:(th + 1) * THF],
                             start=(g == 0), stop=(g == G - 1))
                for th in range(2):
                    ins = q.matmul(ps_s2[th][:], onesb_s[:], sq_r[g % 2][:, th * THF:(th + 1) * THF],
                                   start=(g == 0), stop=(g == G - 1))
                sPE.inc(ins)
                ln_evt[(g, "mm")] = sPE.n
        stats_done = sPE.n

        col = lambda i, th: st1[:, (7 * th + i) * THF:(7 * th + i + 1) * THF]
        ep_ready = []
        for th in range(2):
            mean_c, e2_c, msq_c, var_c, lnv_c, rst_c, mr_c = [col(i, th) for i in range(7)]

            @block.scalar
            def _(q, th=th, mean_c=mean_c, e2_c=e2_c, msq_c=msq_c):
                q.wait_ge(sPE.h, stats_done)
                sAC.inc(q.activation(out=mean_c, in_=ps_s1[th][:], func=AF.Copy, scale=1.0 / D))
                q.activation(out=e2_c, in_=ps_s2[th][:], func=AF.Copy, scale=1.0 / D)
                q.wait_ge(sAC.h, sAC.n)
                sAC.inc(q.activation(out=msq_c, in_=mean_c, func=AF.Square))
            n1 = sAC.n

            @block.vector
            def _(q, n1=n1, var_c=var_c, e2_c=e2_c, msq_c=msq_c):
                q.wait_ge(sAC.h, n1)
                sDV.inc(q.tensor_tensor(out=var_c, in0=e2_c, in1=msq_c, op=OP.subtract))
            n2 = sDV.n

            @block.scalar
            def _(q, n2=n2, lnv_c=lnv_c, var_c=var_c, rst_c=rst_c):
                q.wait_ge(sDV.h, n2)
                sAC.inc(q.activation(out=lnv_c, in_=var_c, func=AF.Ln, bias=np.float32(EPS)))
                q.wait_ge(sAC.h, sAC.n)
                sAC.inc(q.activation(out=rst_c, in_=lnv_c, func=AF.Exp, scale=-0.5))
            n3 = sAC.n

            @block.vector
            def _(q, n3=n3, mr_c=mr_c, mean_c=mean_c, rst_c=rst_c):
                q.wait_ge(sAC.h, n3)
                sDV.inc(q.tensor_tensor(out=mr_c, in0=mean_c, in1=rst_c, op=OP.mult))
            n4 = sDV.n

            @block.tensor
            def _(q, th=th, n3=n3, n4=n4, rst_c=rst_c, mr_c=mr_c):
                q.wait_ge(sAC.h, n3)
                q.matmul(ps_A[th][:], onesp_s[:], rst_c, start=True, stop=True)
                q.wait_ge(sDV.h, n4)
                sPE.inc(q.matmul(ps_B[th][:], onesn_s[:], mr_c, start=True, stop=True))
            ep_ready.append(sPE.n)

        xn_tile_done = {}
        @block.vector
        def _(q):
            mul_cnt = {}
            add_cnt = xn_tile_done
            for th in range(2):
                q.wait_ge(sPE.h, ep_ready[th])
                for g in range(G):
                    xs = x_s[:, g * T + th * THF: g * T + (th + 1) * THF]
                    if (th, g - 4) in add_cnt:
                        q.wait_ge(sDV.h, add_cnt[(th, g - 4)])
                    sDV.inc(q.tensor_tensor(out=xtmp[g % 4][:], in0=xs, in1=ps_A[th][:], op=OP.mult))
                    mul_cnt[(th, g)] = sDV.n
                    gp = g - 1
                    if gp >= 0:
                        q.wait_ge(sDV.h, mul_cnt[(th, gp)])
                        ins = q.tensor_tensor(out=xn[:, gp * T + th * THF: gp * T + (th + 1) * THF],
                                              in0=xtmp[gp % 4][:], in1=ps_B[th][:], op=OP.add)
                        sDV.inc(ins)
                        add_cnt[(th, gp)] = sDV.n
                q.wait_ge(sDV.h, mul_cnt[(th, G - 1)])
                ins = q.tensor_tensor(out=xn[:, (G - 1) * T + th * THF: (G - 1) * T + (th + 1) * THF],
                                      in0=xtmp[(G - 1) % 4][:], in1=ps_B[th][:], op=OP.add)
                sDV.inc(ins)
                add_cnt[(th, G - 1)] = sDV.n
        xn_done = sDV.n

        if "xn" in taps:
            tp = mktap("xn", G * T, BF16)
            @block.sync
            def _(q, tp=tp):
                q.wait_ge(sDV.h, xn_done)
                tapS.put(q.dma_start(out=tp, in_=xn[:]))

        es_ln.close()
        ps_ln.close()
        if stop_after == "ln":
            finish_early()
            return nc

        es_kv = contextlib.ExitStack()
        k_sb = es_kv.enter_context(sb("k_sb", [128, H * T], BF16))
        states = [es_kv.enter_context(sb(f"states{i}", [128, H * 128], F32)) for i in range(2)]
        kT_sb = es_kv.enter_context(sb("kT_sb", [128, H * 8 * 128], BF16))
        v_sb = es_kv.enter_context(sb("v_sb", [128, 8 * D], BF16))
        wqbuf = [es_kv.enter_context(sb(f"wqbuf{i}", [128, D], BF16)) for i in range(2)]
        wobuf = [es_kv.enter_context(sb(f"wobuf{i}", [128, 8 * 128], BF16)) for i in range(4)]
        es_v = contextlib.ExitStack()
        wvbuf = [es_v.enter_context(sb(f"wvbuf{i}", [128, 4 * THF], BF16)) for i in range(4)]
        es_kp = contextlib.ExitStack()
        wkbuf = [es_kp.enter_context(sb(f"wkbuf{i}", [128, D], BF16)) for i in range(3)]
        kpre = [es_kp.enter_context(sb(f"kpre{i}", [128, THF], BF16)) for i in range(2)]
        e1b = es_kp.enter_context(sb("e1b", [128, THF], F32))
        e2b = es_kp.enter_context(sb("e2b", [128, THF], F32))

        ps_kv1 = contextlib.ExitStack()
        ps_k = [ps_kv1.enter_context(nc.psum_tensor(f"ps_k{i}", [128, THF], F32)) for i in range(4)]
        ps_f1 = ps_kv1.enter_context(nc.psum_tensor("ps_f1", [128, THF], F32))
        ps_f2 = ps_kv1.enter_context(nc.psum_tensor("ps_f2", [128, THF], F32))
        ps_tr = [ps_kv1.enter_context(nc.psum_tensor(f"ps_tr{i}", [128, 128], BF16)) for i in range(2)]

        wk_ready = []
        kv_evt = {}

        def kv_back(jt):
            for th in range(2):
                @block.scalar
                def _(q, jt=jt, th=th):
                    q.wait_ge(sPE.h, kv_evt[(jt, th, "kmm")])
                    if (jt - 1, th, "fmm") in kv_evt:
                        q.wait_ge(sPE.h, kv_evt[(jt - 1, th, "fmm")])
                    ins = q.activation(out=kpre[th][:], in_=ps_k[2 * (jt % 2) + th][:], func=AF.Copy)
                    sAC.inc(ins)
                    kv_evt[(jt, th, "kpre_ac")] = sAC.n

                @block.tensor
                def _(q, jt=jt, th=th):
                    q.wait_ge(sAC.h, kv_evt[(jt, th, "kpre_ac")])
                    prev = kv_evt.get("fmap_dv")
                    if prev is not None:
                        q.wait_ge(sDV.h, prev)
                    q.matmul(ps_f1[:], fk1T_s[:], kpre[th][:], start=True, stop=True)
                    sPE.inc(q.matmul(ps_f2[:], fk2T_s[:], kpre[th][:], start=True, stop=True))
                    kv_evt[(jt, th, "fmm")] = sPE.n

                @block.vector
                def _(q, jt=jt, th=th):
                    q.wait_ge(sPE.h, kv_evt[(jt, th, "fmm")])
                    q.wait_ge(sDV.h, sDV.n)
                    q.tensor_scalar(out=e1b[:], in0=ps_f1[:], scalar1=fk1b_s[:],
                                    scalar2=None, op0=OP.add)
                    sDV.inc(q.tensor_scalar(out=e2b[:], in0=ps_f2[:], scalar1=fk2b_s[:],
                                            scalar2=None, op0=OP.add))
                    q.wait_ge(sDV.h, sDV.n)
                    ins = q.tensor_tensor(
                        out=k_sb[:, jt * T + th * THF: jt * T + (th + 1) * THF],
                        in0=e1b[:], in1=e2b[:], op=OP.mult)
                    sDV.inc(ins)
                    kv_evt["fmap_dv"] = sDV.n
                    kv_evt[(jt, th, "k")] = sDV.n

            for tt in range(8):
                @block.tensor
                def _(q, jt=jt, tt=tt):
                    if tt == 0:
                        q.wait_ge(sDV.h, kv_evt[(jt, 1, "k")])
                    pk = (jt, tt - 2, "trev") if tt >= 2 else (jt - 1, tt + 6, "trev")
                    if pk in kv_evt:
                        q.wait_ge(sDV.h, kv_evt[pk])
                    ins = q.transpose(ps_tr[tt % 2][:],
                                      k_sb[:, jt * T + tt * 128: jt * T + (tt + 1) * 128],
                                      ident_s[:])
                    sPE.inc(ins)
                    kv_evt[(jt, tt, "tr")] = sPE.n

                @block.vector
                def _(q, jt=jt, tt=tt):
                    q.wait_ge(sPE.h, kv_evt[(jt, tt, "tr")])
                    ins = q.tensor_copy(
                        out=kT_sb[:, (jt * 8 + tt) * 128:(jt * 8 + tt + 1) * 128],
                        in_=ps_tr[tt % 2][:])
                    sDV.inc(ins)
                    kv_evt[(jt, tt, "trev")] = sDV.n

        for jt in range(G):
            @block.gpsimd
            def _(q, jt=jt):
                if jt < 3:
                    q.wait_ge(sDV.h, xn_done)
                else:
                    q.wait_ge(sPE.h, kv_evt[(jt - 3, 1, "kmm")])
                wk_ready.append(wkS.put(q.dma_start(out=wkbuf[jt % 3][:], in_=wk_st[jt])))

            @block.tensor
            def _(q, jt=jt):
                wtok(q, wk_ready[jt])
                if jt == 0:
                    q.wait_ge(sW.h, n_fmapw)
                for th in range(2):
                    if (jt - 2, th, "kpre_ac") in kv_evt:
                        q.wait_ge(sAC.h, kv_evt[(jt - 2, th, "kpre_ac")])
                    for g in range(G):
                        if jt == 0:
                            q.wait_ge(sDV.h, xn_tile_done[(th, g)])
                        ins = q.matmul(ps_k[2 * (jt % 2) + th][:], wkbuf[jt % 3][:, g * 128:(g + 1) * 128],
                                       xn[:, g * T + th * THF: g * T + (th + 1) * THF],
                                       start=(g == 0), stop=(g == G - 1))
                    sPE.inc(ins)
                    kv_evt[(jt, th, "kmm")] = sPE.n

            if jt - 1 >= 0:
                kv_back(jt - 1)
        kv_back(G - 1)
        k_done = sDV.n
        kT_done = sDV.n
        kmm_all = sPE.n
        ps_kv1.close()
        es_kp.close()

        ps_v = contextlib.ExitStack()
        ps_vt = [ps_v.enter_context(nc.psum_tensor(f"ps_v{t}", [128, THF], F32)) for t in range(8)]

        wv_ready = []
        for jb in range(4):
            for gq in range(4):
                i = jb * 4 + gq
                @block.gpsimd
                def _(q, jb=jb, gq=gq, i=i):
                    if i < 4:
                        q.wait_ge(sDV.h, xn_done)
                    else:
                        pi = divmod(4 * (i - 4) + 3, G)
                        q.wait_ge(sPE.h, kv_evt[(pi[0], pi[1], "vmm")])
                    ins = q.dma_start(out=wvbuf[i % 4][:],
                                      in_=wv_st[jb][:, gq * 4 * THF:(gq + 1) * 4 * THF])
                    wv_ready.append(wvS.put(ins))

                for gg in range(4):
                    g = gq * 4 + gg
                    @block.tensor
                    def _(q, jb=jb, gq=gq, gg=gg, g=g, i=i):
                        if jb == 0 and g == 0:
                            q.wait_ge(sAC.h, sAC.n)
                            q.wait_ge(sDV.h, sDV.n)
                        if g == 0 and jb > 0:
                            q.wait_ge(sAC.h, kv_evt[(jb - 1, "vev")])
                        if gg == 0:
                            wtok(q, wv_ready[i])
                        for tt in range(8):
                            ins = q.matmul(ps_vt[tt][:],
                                           xn[:, g * T + tt * 128: g * T + (tt + 1) * 128],
                                           wvbuf[i % 4][:, gg * THF:(gg + 1) * THF],
                                           start=(g == 0), stop=(g == G - 1))
                        sPE.inc(ins)
                        kv_evt[(jb, g, "vmm")] = sPE.n

            @block.scalar
            def _(q, jb=jb):
                q.wait_ge(sPE.h, kv_evt[(jb, G - 1, "vmm")])
                for tt in range(8):
                    ins = q.activation(out=v_sb[:, tt * D + jb * THF: tt * D + (jb + 1) * THF],
                                       in_=ps_vt[tt][:], func=AF.Copy)
                sAC.inc(ins)
                kv_evt[(jb, "vev")] = sAC.n
        v_done = sAC.n
        ps_v.close()
        es_v.close()
        es_q2 = contextlib.ExitStack()
        q_sb = es_q2.enter_context(sb("q_sb", [128, H * THF], BF16))
        qpre = [es_q2.enter_context(sb(f"qpre{i}", [128, THF], BF16)) for i in range(2)]
        qe1 = es_q2.enter_context(sb("qe1", [128, THF], F32))
        qe2 = es_q2.enter_context(sb("qe2", [128, THF], F32))

        es_S = contextlib.ExitStack()
        Ssb = es_S.enter_context(sb("Ssb", [128, H * 128], F32))
        ps_Sx = contextlib.ExitStack()
        ps_S = [ps_Sx.enter_context(nc.psum_tensor(f"ps_S{i}", [128, 128], F32)) for i in range(2)]
        for h in range(H):
            @block.tensor
            def _(q, h=h):
                if h == 0:
                    q.wait_ge(sDV.h, kT_done)
                    q.wait_ge(sAC.h, v_done)
                if (h - 2, "Sev") in kv_evt:
                    q.wait_ge(sDV.h, kv_evt[(h - 2, "Sev")])
                for tt in range(8):
                    ins = q.matmul(ps_S[h % 2][:],
                                   kT_sb[:, (h * 8 + tt) * 128:(h * 8 + tt + 1) * 128],
                                   v_sb[:, tt * D + h * 128: tt * D + (h + 1) * 128],
                                   start=(tt == 0), stop=(tt == 7))
                sPE.inc(ins)
                kv_evt[(h, "Smm")] = sPE.n

            @block.vector
            def _(q, h=h):
                q.wait_ge(sPE.h, kv_evt[(h, "Smm")])
                ins = q.tensor_copy(out=Ssb[:, h * 128:(h + 1) * 128], in_=ps_S[h % 2][:])
                sDV.inc(ins)
                kv_evt[(h, "Sev")] = sDV.n
        S_done = sDV.n
        ps_Sx.close()

        @block.sync
        def _(q):
            q.wait_ge(sDV.h, S_done)
            sSTO.inc(q.dma_start(out=cc_in.rearrange("(h d) e -> d h e", h=H), in_=Ssb[:]), 16)
        ccin_done = sSTO.n

        wq_ready = {}
        @block.gpsimd
        def _(q):
            for jt in range(2):
                wq_ready[(0, jt)] = wqS.put(q.dma_start(out=wqbuf[jt % 2][:], in_=wq_st[jt]))
            q.wait_ge(sSTO.h, ccin_done)
            sCC.inc(q.collective_compute(
                "AllGather", OP.bypass,
                ins=[cc_in.opt()], outs=[cc_out.opt()],
                replica_groups=[list(range(NCORES))],
            ))
        cc_done = sCC.n

        gbuf = [es_S.enter_context(sb(f"gbuf{i}", [128, H * 64], F32)) for i in range(2)]
        gview = cc_out.rearrange("(r hf hd d) e -> r hf d hd e", r=NCORES, hf=2, d=128)
        pfx_evt = {}
        HW = H * 64
        for r in range(NCORES):
            for hf in range(2):
                i = 2 * r + hf
                @block.sync
                def _(q, r=r, hf=hf, i=i):
                    if i == 0:
                        q.wait_ge(sCC.h, cc_done)
                    if (i - 2, "pfx") in pfx_evt:
                        q.wait_ge(sDV.h, pfx_evt[(i - 2, "pfx")])
                    pfx_evt[(i, "ld")] = gS.put(q.dma_start(out=gbuf[i % 2][:], in_=gview[r, hf]))

                @block.vector
                def _(q, r=r, hf=hf, i=i):
                    wtok(q, pfx_evt[(i, "ld")])
                    q.wait_ge(sDV.h, sDV.n)
                    csl = slice(hf * HW, (hf + 1) * HW)
                    if r == 0:
                        ins = q.tensor_scalar(out=states[1][:, csl], in0=gbuf[i % 2][:],
                                              scalar1=wsel_s[:, 0:1], scalar2=None, op0=OP.mult)
                    else:
                        sDV.inc(q.tensor_scalar(out=gbuf[i % 2][:], in0=gbuf[i % 2][:],
                                                 scalar1=wsel_s[:, r:r + 1], scalar2=None, op0=OP.mult))
                        q.wait_ge(sDV.h, sDV.n)
                        s_, t_ = (states[1], states[0]) if r % 2 == 1 else (states[0], states[1])
                        ins = q.tensor_tensor(out=t_[:, csl], in0=s_[:, csl],
                                              in1=gbuf[i % 2][:], op=OP.add)
                    sDV.inc(ins)
                    pfx_evt[(i, "pfx")] = sDV.n
        pfx_done = sDV.n
        es_S.close()

        guard_pre_scan = []
        for nm, buf, wid, dty, wsem, wn in [
                ("k", k_sb, H * T, BF16, sDV, k_done),
                ("v", v_sb, 8 * D, BF16, sAC, v_done),
                ("kT", kT_sb, H * 8 * 128, BF16, sDV, kT_done),
                ("states", states[0], H * 128, F32, sDV, pfx_done)]:
            if nm in taps:
                tp = mktap(nm, wid, dty)
                @block.sync
                def _(q, tp=tp, buf=buf, wsem=wsem, wn=wn):
                    q.wait_ge(wsem.h, wn)
                    guard_pre_scan.append(tapS.put(q.dma_start(out=tp, in_=buf[:])))

        if stop_after == "kv":
            finish_early()
            es_kv.close()
            return nc

        es_qs = contextlib.ExitStack()
        sTm_sb = es_qs.enter_context(sb("sTm_sb", [128, 256], BF16))
        stb = [es_qs.enter_context(sb(f"stb{i}", [128, 512], BF16)) for i in range(2)]
        o_bf = es_qs.enter_context(sb("o_bf", [128, 4 * THF], BF16))
        o_grp = es_qs.enter_context(sb("o_grp", [128, 8 * THF], BF16))
        sqo = [es_qs.enter_context(sb("sqo0", [128, THF], BF16))] * 2
        lnr = es_qs.enter_context(sb("lnr", [1, THF], F32))
        park = [es_qs.enter_context(sb(f"park{i}", [128, THF], F32)) for i in range(2)]

        sc = {}
        wo_ready = {}
        guard_q0 = []

        def prevchunk(t2, grp, n, k):
            n2 = n - k
            if n2 >= 0:
                return (t2, grp, n2)
            if grp > 0:
                return (t2, grp - 1, n2 + 8)
            if t2 > 0:
                return (t2 - 1, 3, n2 + 8)
            return None

        for t2 in range(2):
            ps_q = contextlib.ExitStack()
            ps_qp = [ps_q.enter_context(nc.psum_tensor(f"ps_qp{t2}_{i}", [128, THF], F32)) for i in range(2)]
            ps_qf1 = ps_q.enter_context(nc.psum_tensor(f"ps_qf1{t2}", [128, THF], F32))
            ps_qf2 = ps_q.enter_context(nc.psum_tensor(f"ps_qf2{t2}", [128, THF], F32))
            for jt in range(G):
                if not (t2 == 0 and jt < 2):
                    @block.gpsimd
                    def _(q, t2=t2, jt=jt):
                        pk = (t2, jt - 2, "qmm") if jt >= 2 else (t2 - 1, G + jt - 2, "qmm")
                        q.wait_ge(sPE.h, sc[pk])
                        wq_ready[(t2, jt)] = wqS.put(q.dma_start(out=wqbuf[jt % 2][:], in_=wq_st[jt]))

                @block.tensor
                def _(q, t2=t2, jt=jt):
                    wtok(q, wq_ready[(t2, jt)])
                    if jt == 0 and t2 == 0:
                        q.wait_ge(sAC.h, v_done)
                        q.wait_ge(sDV.h, S_done)
                    pk = (t2, jt - 2, "qpre_ac") if jt >= 2 else (t2 - 1, G + jt - 2, "qpre_ac")
                    if pk in sc:
                        q.wait_ge(sAC.h, sc[pk])
                    for g in range(G):
                        ins = q.matmul(ps_qp[jt % 2][:], wqbuf[jt % 2][:, g * 128:(g + 1) * 128],
                                       xn[:, g * T + t2 * THF: g * T + (t2 + 1) * THF],
                                       start=(g == 0), stop=(g == G - 1))
                    sPE.inc(ins)
                    sc[(t2, jt, "qmm")] = sPE.n

                @block.scalar
                def _(q, t2=t2, jt=jt):
                    q.wait_ge(sPE.h, sc[(t2, jt, "qmm")])
                    pk = (t2, jt - 2, "qfmm") if jt >= 2 else (t2 - 1, G + jt - 2, "qfmm")
                    if pk in sc:
                        q.wait_ge(sPE.h, sc[pk])
                    sAC.inc(q.activation(out=qpre[jt % 2][:], in_=ps_qp[jt % 2][:], func=AF.Copy))
                    sc[(t2, jt, "qpre_ac")] = sAC.n

                @block.tensor
                def _(q, t2=t2, jt=jt):
                    q.wait_ge(sAC.h, sc[(t2, jt, "qpre_ac")])
                    prev = sc.get("qfmap_dv")
                    if prev is not None:
                        q.wait_ge(sDV.h, prev)
                    q.matmul(ps_qf1[:], fq1T_s[:], qpre[jt % 2][:], start=True, stop=True)
                    sPE.inc(q.matmul(ps_qf2[:], fq2T_s[:], qpre[jt % 2][:], start=True, stop=True))
                    sc[(t2, jt, "qfmm")] = sPE.n

                @block.vector
                def _(q, t2=t2, jt=jt):
                    q.wait_ge(sPE.h, sc[(t2, jt, "qfmm")])
                    q.wait_ge(sDV.h, sDV.n)
                    if t2 == 1:
                        q.wait_ge(sPE.h, sc[(0, jt // 4, 7, "omm")])
                        q.wait_ge(sAC.h, sc[(0, 3, 3, "ln")])
                        if guard_q0:
                            wtok(q, guard_q0)
                    q.tensor_scalar(out=qe1[:], in0=ps_qf1[:], scalar1=fq1b_s[:],
                                    scalar2=QSCALE, op0=OP.add, op1=OP.mult)
                    sDV.inc(q.tensor_scalar(out=qe2[:], in0=ps_qf2[:], scalar1=fq2b_s[:],
                                            scalar2=None, op0=OP.add))
                    q.wait_ge(sDV.h, sDV.n)
                    ins = q.tensor_tensor(out=q_sb[:, jt * THF:(jt + 1) * THF],
                                          in0=qe1[:], in1=qe2[:], op=OP.mult)
                    sDV.inc(ins)
                    sc["qfmap_dv"] = sDV.n
                    sc[(t2, jt, "q")] = sDV.n
            q_half_done = sDV.n
            ps_q.close()

            if f"q{t2}" in taps:
                tp = mktap(f"q{t2}", H * THF, BF16)
                @block.sync
                def _(q, tp=tp, n=q_half_done):
                    q.wait_ge(sDV.h, n)
                    tok = tapS.put(q.dma_start(out=tp, in_=q_sb[:]))
                    if t2 == 0:
                        guard_q0.append(tok)

            ps_sc = contextlib.ExitStack()
            ps_sT = [ps_sc.enter_context(nc.psum_tensor(f"ps_sT{t2}_{i}", [128, 256], F32)) for i in range(2)]
            ps_o = [ps_sc.enter_context(nc.psum_tensor(f"ps_o{t2}_{i}", [128, THF], F32)) for i in range(2)]
            ps_kvo = [ps_sc.enter_context(nc.psum_tensor(f"ps_kv{t2}_{i}", [128, 512], F32)) for i in range(2)]
            ps_rb = [ps_sc.enter_context(nc.psum_tensor(f"ps_rb{t2}_{i}", [128, THF], F32))
                     for i in range(2)]

            for grp in range(4):
                hs = [4 * grp + i for i in range(4)]
                gc = slice(grp * 512, (grp + 1) * 512)

                @block.vector
                def _(q, t2=t2, grp=grp, gc=gc):
                    if t2 == 0 and grp == 0:
                        q.wait_ge(sDV.h, pfx_done)
                        if guard_pre_scan:
                            wtok(q, guard_pre_scan)
                    if t2 == 1:
                        q.wait_ge(sDV.h, sc[(0, grp, 7, "caa")])
                    pg = (t2, grp - 1) if grp > 0 else ((t2 - 1, 3) if t2 > 0 else None)
                    if pg is not None:
                        q.wait_ge(sPE.h, sc[(pg[0], pg[1], 7, "omm")])
                    ins = q.tensor_copy(out=stb[0][:], in_=states[0][:, gc])
                    sDV.inc(ins)
                    sc[(t2, grp, "init")] = sDV.n

                for n in range(8):
                    m = t2 * 8 + n
                    par = m % 2
                    base = (m * CH) % 128
                    tt_idx = m // 2
                    last = (m == 15)

                    @block.tensor
                    def _(q, t2=t2, grp=grp, n=n, par=par, base=base, tt_idx=tt_idx,
                          hs=hs, last=last):
                        if n == 0:
                            q.wait_ge(sDV.h, q_half_done)
                            q.wait_ge(sAC.h, sc.get((t2, G - 1, "qpre_ac"), 0))
                            if t2 == 0 and grp == 0:
                                q.wait_ge(sDV.h, pfx_done)
                                q.wait_ge(sDV.h, k_done)
                                q.wait_ge(sAC.h, v_done)
                        if not last:
                            pk = prevchunk(t2, grp, n, 2)
                            if pk is not None and (pk + ("add",)) in sc:
                                q.wait_ge(sDV.h, sc[pk + ("add",)])
                            for i, h in enumerate(hs):
                                ins = q.matmul(
                                    ps_kvo[par][:, i * 128:(i + 1) * 128],
                                    kT_sb[base:base + 64,
                                          (h * 8 + tt_idx) * 128:(h * 8 + tt_idx + 1) * 128],
                                    v_sb[base:base + 64,
                                         tt_idx * D + h * 128: tt_idx * D + (h + 1) * 128],
                                    start=True, stop=True)
                            sPE.inc(ins)
                            sc[(t2, grp, n, "kvmm")] = sPE.n
                        pk = prevchunk(t2, grp, n, 2)
                        if pk is not None and (pk + ("mask",)) in sc:
                            q.wait_ge(sDV.h, sc[pk + ("mask",)])
                        for i, h in enumerate(hs):
                            ins = q.matmul(
                                ps_sT[par][base:base + 64, i * 64:(i + 1) * 64],
                                k_sb[:, h * T + t2 * THF + n * CH: h * T + t2 * THF + (n + 1) * CH],
                                q_sb[:, h * THF + n * CH: h * THF + (n + 1) * CH],
                                start=True, stop=True)
                        sPE.inc(ins)
                        sc[(t2, grp, n, "sT")] = sPE.n

                    if not last:
                        @block.vector
                        def _(q, t2=t2, grp=grp, n=n, par=par, gc=gc):
                            q.wait_ge(sPE.h, sc[(t2, grp, n, "kvmm")])
                            q.wait_ge(sDV.h, sDV.n)
                            sDV.inc(q.tensor_tensor(out=states[1 - par][:, gc],
                                                    in0=states[par][:, gc],
                                                    in1=ps_kvo[par][:], op=OP.add))
                            sc[(t2, grp, n, "add")] = sDV.n

                    @block.vector
                    def _(q, t2=t2, grp=grp, n=n, par=par, base=base):
                        q.wait_ge(sPE.h, sc[(t2, grp, n, "sT")])
                        pk = prevchunk(t2, grp, n, 2)
                        if pk is not None and (pk + ("omm",)) in sc:
                            q.wait_ge(sPE.h, sc[pk + ("omm",)])
                        ins = q.tensor_tensor(out=sTm_sb[base:base + 64, :],
                                              in0=ps_sT[par][base:base + 64, :],
                                              in1=mask_s[:], op=OP.mult)
                        sDV.inc(ins)
                        sc[(t2, grp, n, "mask")] = sDV.n

                    if not last:
                        @block.vector
                        def _(q, t2=t2, grp=grp, n=n, par=par, gc=gc):
                            q.wait_ge(sDV.h, sc[(t2, grp, n, "add")])
                            ins = q.tensor_copy(out=stb[1 - par][:], in_=states[1 - par][:, gc])
                            sDV.inc(ins)
                            sc[(t2, grp, n, "caa")] = sDV.n
                    else:
                        sc[(t2, grp, n, "caa")] = sc[(t2, grp, n - 1, "caa")]

                    @block.tensor
                    def _(q, t2=t2, grp=grp, n=n, par=par, base=base, tt_idx=tt_idx,
                          hs=hs, last=last):
                        if grp % 2 == 0 and n < 2:
                            ppk = (t2, grp // 2 - 1, "park_ac_last") if grp >= 2 else (t2 - 1, 1, "park_ac_last")
                            if ppk in sc:
                                q.wait_ge(sAC.h, sc[ppk])
                        pk = prevchunk(t2, grp, n, 2)
                        if pk is not None and (pk + ("oev",)) in sc:
                            q.wait_ge(sAC.h, sc[pk + ("oev",)])
                        ck = sc[(t2, grp, "init")] if n == 0 else sc[(t2, grp, n - 1, "caa")]
                        q.wait_ge(sDV.h, ck)
                        q.wait_ge(sDV.h, sc[(t2, grp, n, "mask")])
                        for i, h in enumerate(hs):
                            q.matmul(ps_o[par][:, i * 64:(i + 1) * 64],
                                     stb[par][:, i * 128:(i + 1) * 128],
                                     q_sb[:, h * THF + n * CH: h * THF + (n + 1) * CH],
                                     start=True, stop=False)
                            ins = q.matmul(ps_o[par][:, i * 64:(i + 1) * 64],
                                           v_sb[base:base + 64,
                                                tt_idx * D + h * 128: tt_idx * D + (h + 1) * 128],
                                           sTm_sb[base:base + 64, i * 64:(i + 1) * 64],
                                           start=False, stop=True)
                        sPE.inc(ins)
                        sc[(t2, grp, n, "omm")] = sPE.n

                    @block.scalar
                    def _(q, t2=t2, grp=grp, n=n, par=par):
                        q.wait_ge(sPE.h, sc[(t2, grp, n, "omm")])
                        if n == 0:
                            pg = (t2, grp - 1) if grp > 0 else ((t2 - 1, 3) if t2 > 0 else None)
                            if pg is not None:
                                q.wait_ge(sDV.h, sc[(pg[0], pg[1], 3, "own")])
                        for i in range(4):
                            ins = q.activation(out=o_bf[:, i * THF + n * CH: i * THF + (n + 1) * CH],
                                               in_=ps_o[par][:, i * 64:(i + 1) * 64],
                                               func=AF.Copy)
                        sAC.inc(ins)
                        sc[(t2, grp, n, "oev")] = sAC.n

                def rms_sq(i, t2=t2, grp=grp):
                    @block.scalar
                    def _(q, t2=t2, grp=grp, i=i):
                        if i == 0:
                            q.wait_ge(sAC.h, sc[(t2, grp, 7, "oev")])
                        pk = (t2, grp, i - 1, "ssmm")
                        if pk in sc:
                            q.wait_ge(sPE.h, sc[pk])
                        ins = q.activation(out=sqo[i % 2][:], in_=o_bf[:, i * THF:(i + 1) * THF],
                                           func=AF.Square)
                        sAC.inc(ins)
                        sc[(t2, grp, i, "sq")] = sAC.n

                def rms_ssmm(i, t2=t2, grp=grp):
                    @block.tensor
                    def _(q, t2=t2, grp=grp, i=i):
                        q.wait_ge(sAC.h, sc[(t2, grp, i, "sq")])
                        pk = (t2, grp, i - 2, "own")
                        if pk in sc:
                            q.wait_ge(sDV.h, sc[pk])
                        elif i < 2 and (t2, grp - 1, 2 + i, "own") in sc:
                            q.wait_ge(sDV.h, sc[(t2, grp - 1, 2 + i, "own")])
                        ins = q.matmul(ps_rb[i % 2][0:1, :], onesb_s[:], sqo[i % 2][:],
                                       start=True, stop=True)
                        sPE.inc(ins)
                        sc[(t2, grp, i, "ssmm")] = sPE.n

                def rms_ln(i, t2=t2, grp=grp):
                    @block.scalar
                    def _(q, t2=t2, grp=grp, i=i):
                        q.wait_ge(sPE.h, sc[(t2, grp, i, "ssmm")])
                        if (t2, grp, i - 1, "rbmm") in sc:
                            q.wait_ge(sPE.h, sc[(t2, grp, i - 1, "rbmm")])
                        sAC.inc(q.activation(out=qe2[0:1, :], in_=ps_rb[i % 2][0:1, :], func=AF.Ln,
                                             scale=1.0 / DK, bias=np.float32(EPS)))
                        q.wait_ge(sAC.h, sAC.n)
                        ins = q.activation(out=lnr[:], in_=qe2[0:1, :],
                                           func=AF.Exp, scale=-0.5)
                        sAC.inc(ins)
                        sc[(t2, grp, i, "ln")] = sAC.n

                def rms_rbmm(i, t2=t2, grp=grp):
                    @block.tensor
                    def _(q, t2=t2, grp=grp, i=i):
                        q.wait_ge(sAC.h, sc[(t2, grp, i, "ln")])
                        ins = q.matmul(ps_rb[i % 2][:], onesp_s[:], lnr[:],
                                       start=True, stop=True)
                        sPE.inc(ins)
                        sc[(t2, grp, i, "rbmm")] = sPE.n

                def rms_own(i, t2=t2, grp=grp):
                    slot = (grp % 2) * 4 + i
                    @block.vector
                    def _(q, t2=t2, grp=grp, i=i, slot=slot):
                        q.wait_ge(sPE.h, sc[(t2, grp, i, "rbmm")])
                        q.wait_ge(sDV.h, sDV.n)
                        pp = (t2, grp // 2 - 1, "wo_mm_last") if grp >= 2 else (t2 - 1, 1, "wo_mm_last")
                        if pp in sc:
                            q.wait_ge(sPE.h, sc[pp])
                        sDV.inc(q.tensor_scalar(out=qe1[:], in0=o_bf[:, i * THF:(i + 1) * THF],
                                                 scalar1=normw_s[:], scalar2=None, op0=OP.mult))
                        q.wait_ge(sDV.h, sDV.n)
                        ins = q.tensor_tensor(out=o_grp[:, slot * THF:(slot + 1) * THF],
                                              in0=qe1[:], in1=ps_rb[i % 2][:], op=OP.mult)
                        sDV.inc(ins)
                        sc[(t2, grp, i, "own")] = sDV.n

                rms_sq(0)
                rms_ssmm(0)
                rms_sq(1)
                rms_ln(0)
                rms_ssmm(1)
                rms_rbmm(0)
                rms_sq(2)
                rms_ln(1)
                rms_own(0)
                rms_ssmm(2)
                rms_rbmm(1)
                rms_sq(3)
                rms_ln(2)
                rms_own(1)
                rms_ssmm(3)
                rms_rbmm(2)
                rms_ln(3)
                rms_own(2)
                rms_rbmm(3)
                rms_own(3)

                if grp % 2 == 1:
                    p = grp // 2
                    hs8 = list(range(8 * p, 8 * p + 8))
                    for jt in range(G):
                        w = (t2 * 2 + p) * G + jt
                        @block.gpsimd
                        def _(q, t2=t2, p=p, jt=jt, w=w):
                            if w >= 4:
                                q.wait_ge(sPE.h, sc[("wo_mm", w - 4)])
                            wo_ready[w] = woS.put(q.dma_start(out=wobuf[w % 4][:], in_=wo_st[p, jt]))

                        @block.tensor
                        def _(q, t2=t2, grp=grp, p=p, jt=jt, w=w, hs8=hs8):
                            wtok(q, wo_ready[w])
                            if jt == 0:
                                q.wait_ge(sDV.h, sc[(t2, grp, 3, "own")])
                                q.wait_ge(sAC.h, sc[(t2, grp, 7, "oev")])
                            if jt >= 2:
                                q.wait_ge(sAC.h, sc[(t2, p, jt - 2, "park_ac")])
                            for s in range(8):
                                ins = q.matmul(ps_o[jt % 2][:],
                                               wobuf[w % 4][:, s * 128:(s + 1) * 128],
                                               o_grp[:, s * THF:(s + 1) * THF],
                                               start=(s == 0), stop=(s == 7))
                            sPE.inc(ins)
                            sc[("wo_mm", w)] = sPE.n
                            sc[(t2, p, jt, "wo_mm")] = sPE.n
                            if jt == G - 1:
                                sc[(t2, p, "wo_mm_last")] = sPE.n
                                if p == 1:
                                    sc[(t2, G - 1, "womm")] = sPE.n

                        @block.scalar
                        def _(q, t2=t2, p=p, jt=jt):
                            q.wait_ge(sPE.h, sc[(t2, p, jt, "wo_mm")])
                            pkst = (t2, p, jt - 2, "park_st") if jt >= 2 else \
                                   ((t2, p - 1, jt + 14, "park_st") if p >= 1 else (t2 - 1, 1, jt + 14, "park_st"))
                            if pkst in sc:
                                wtok(q, sc[pkst])
                            ins = q.activation(out=park[jt % 2][:], in_=ps_o[jt % 2][:], func=AF.Copy)
                            sAC.inc(ins)
                            sc[(t2, p, jt, "park_ac")] = sAC.n
                            if jt == G - 1:
                                sc[(t2, p, "park_ac_last")] = sAC.n

                        @block.sync
                        def _(q, t2=t2, p=p, jt=jt):
                            q.wait_ge(sAC.h, sc[(t2, p, jt, "park_ac")])
                            ins = q.dma_start(
                                out=attnd[p][jt * 128:(jt + 1) * 128, t2 * THF:(t2 + 1) * THF],
                                in_=park[jt % 2][:])
                            sc[(t2, p, jt, "park_st")] = parkS.put(ins)
            owo_done = sDV.n
            ps_sc.close()
        attn_parked = parkS.totals()
        es_qs.close()
        es_q2.close()
        es_kv.close()

        if "attn" in taps:
            tp0 = mktap("attn", T, F32, rows=D)
            tp1 = mktap("attn1", T, F32, rows=D)
            @block.sync
            def _(q, tp0=tp0, tp1=tp1):
                wtok(q, attn_parked)
                tapS.put(q.dma_start(out=tp0, in_=attnd[0]))
                tapS.put(q.dma_start(out=tp1, in_=attnd[1]))

        if stop_after == "attn":
            finish_early()
            return nc

        es_f1 = contextlib.ExitStack()
        w1buf = [es_f1.enter_context(sb(f"w1buf{i}", [128, D], BF16)) for i in range(2)]
        w3buf = [es_f1.enter_context(sb(f"w3buf{i}", [128, D], BF16)) for i in range(2)]
        silb = [es_f1.enter_context(sb(f"silb{i}", [128, THF], F32)) for i in range(2)]
        hidb = [es_f1.enter_context(sb(f"hidb{i}", [128, THF], BF16)) for i in range(2)]
        h1b = [es_f1.enter_context(sb(f"h1b{i}", [128, THF], BF16)) for i in range(2)]

        ps_f1s = contextlib.ExitStack()
        ps_g = [ps_f1s.enter_context(nc.psum_tensor(f"ps_g{i}", [128, THF], F32)) for i in range(2)]
        ps_l = [ps_f1s.enter_context(nc.psum_tensor(f"ps_l{i}", [128, THF], F32)) for i in range(2)]
        ps_h1 = [ps_f1s.enter_context(nc.psum_tensor(f"ps_h1{i}", [128, THF], F32)) for i in range(2)]

        w13_ready = {}
        f1 = {}
        pe_pre_f1 = sPE.n
        dv_pre_f1 = sDV.n
        hid1v = hid1d.rearrange("(p2 j g) t -> p2 j g t", p2=64, j=2, g=64)

        def f1_back(idx):
            pp = idx % 2
            hg, th = divmod(idx, 2)

            @block.tensor
            def _(q, pp=pp, idx=idx):
                q.wait_ge(sDV.h, f1[(idx, "mult")])
                if idx - 2 >= 0:
                    q.wait_ge(sDV.h, f1[(idx - 2, "h1ev")])
                ins = q.matmul(ps_h1[pp][:], h128_s[:], hidb[pp][:], start=True, stop=True)
                sPE.inc(ins)
                f1[(idx, "h128")] = sPE.n

            @block.vector
            def _(q, pp=pp, idx=idx):
                q.wait_ge(sPE.h, f1[(idx, "h128")])
                if idx - 2 >= 0:
                    wtok(q, f1[(idx - 2, "h1st")])
                ins = q.tensor_copy(out=h1b[pp][:], in_=ps_h1[pp][:])
                sDV.inc(ins)
                f1[(idx, "h1ev")] = sDV.n

            @block.sync
            def _(q, hg=hg, th=th, pp=pp, idx=idx):
                q.wait_ge(sDV.h, f1[(idx, "h1ev")])
                ins = q.dma_start(out=hid1v[:, :, hg, th * THF:(th + 1) * THF],
                                  in_=h1b[pp][:])
                f1[(idx, "h1st")] = h1S.put(ins)

        for hg in range(HG):
            @block.gpsimd
            def _(q, hg=hg):
                if hg < 2:
                    q.wait_ge(sPE.h, pe_pre_f1)
                    wtok(q, attn_parked)
                    q.wait_ge(sDV.h, dv_pre_f1)
                else:
                    q.wait_ge(sPE.h, f1[(2 * (hg - 2) + 1, "mm")])
                ia = q.dma_start(out=w1buf[hg % 2][:], in_=w1_st[hg])
                ib = q.dma_start(out=w3buf[hg % 2][:], in_=w3_st[hg])
                w13_ready[hg] = w13S.put2(ia, ib)

            for th in range(2):
                idx = 2 * hg + th
                pp = idx % 2
                @block.tensor
                def _(q, hg=hg, th=th, pp=pp, idx=idx):
                    if idx == 0:
                        q.wait_ge(sAC.h, sAC.n)
                        q.wait_ge(sDV.h, sDV.n)
                    wtok(q, w13_ready[hg])
                    if idx - 2 >= 0:
                        q.wait_ge(sAC.h, f1[(idx - 2, "sil")])
                        q.wait_ge(sDV.h, f1[(idx - 2, "mult")])
                    for g in range(G):
                        q.matmul(ps_g[pp][:], w1buf[hg % 2][:, g * 128:(g + 1) * 128],
                                 xn[:, g * T + th * THF: g * T + (th + 1) * THF],
                                 start=(g == 0), stop=(g == G - 1))
                    for g in range(G):
                        ins = q.matmul(ps_l[pp][:], w3buf[hg % 2][:, g * 128:(g + 1) * 128],
                                       xn[:, g * T + th * THF: g * T + (th + 1) * THF],
                                       start=(g == 0), stop=(g == G - 1))
                    sPE.inc(ins)
                    f1[(idx, "mm")] = sPE.n

                @block.scalar
                def _(q, pp=pp, idx=idx):
                    q.wait_ge(sPE.h, f1[(idx, "mm")])
                    if idx - 2 >= 0:
                        q.wait_ge(sDV.h, f1[(idx - 2, "mult")])
                    ins = q.activation(out=silb[pp][:], in_=ps_g[pp][:], func=AF.Silu)
                    sAC.inc(ins)
                    f1[(idx, "sil")] = sAC.n

                @block.vector
                def _(q, pp=pp, idx=idx):
                    q.wait_ge(sAC.h, f1[(idx, "sil")])
                    if idx - 2 >= 0:
                        q.wait_ge(sPE.h, f1[(idx - 2, "h128")])
                    ins = q.tensor_tensor(out=hidb[pp][:], in0=silb[pp][:],
                                          in1=ps_l[pp][:], op=OP.mult)
                    sDV.inc(ins)
                    f1[(idx, "mult")] = sDV.n

                if idx - 1 >= 0:
                    f1_back(idx - 1)
        f1_back(2 * HG - 1)
        hid1_done = h1S.totals()
        ps_f1s.close()
        es_f1.close()

        if "hid1" in taps:
            tp = mktap("hid1", T, BF16, rows=HG * 128)
            @block.sync
            def _(q, tp=tp):
                wtok(q, hid1_done)
                tapS.put(q.dma_start(out=tp, in_=hid1d))

        if stop_after == "f1":
            finish_early()
            return nc

        es_f2 = contextlib.ExitStack()
        ACC = es_f2.enter_context(sb("ACC", [128, G * T], F32))
        hid2 = es_f2.enter_context(sb("hid2", [128, U * THF], BF16))
        hp2 = [es_f2.enter_context(sb(f"hp2_{i}", [128, 4 * THF], BF16)) for i in range(2)]
        w2buf = [es_f2.enter_context(sb(f"w2buf{i}", [128, 8 * 128], BF16)) for i in range(4)]

        hview = hid1d.rearrange("(u r) t -> r u t", u=U)
        f2 = {}
        w2_ready = {}

        es_out = contextlib.ExitStack()
        x2b = [es_out.enter_context(sb(f"x2b{i}", [128, THF], F32)) for i in range(2)]
        atb = [es_out.enter_context(sb(f"atb{i}", [128, THF], F32)) for i in range(2)]
        at2b = [es_out.enter_context(sb(f"at2b{i}", [128, THF], F32)) for i in range(2)]
        stg = [es_out.enter_context(sb(f"stg{i}", [128, THF], F32)) for i in range(2)]
        oev = {}
        o_i = [0]

        def emit_out(j, acc_ready):
            oi = o_i[0]
            o_i[0] += 1
            cs = slice(j * THF, (j + 1) * THF)
            g, hf = divmod(j, 2)

            @block.sync
            def _(q, j=j, oi=oi, cs=cs, g=g, hf=hf):
                if oi == 0:
                    wtok(q, attn_parked)
                if (oi - 2, "st") in oev:
                    wtok(q, oev[(oi - 2, "st")])
                ia = q.dma_start(out=x2b[oi % 2][:], in_=x_st[:, cs])
                ib = q.dma_start(out=atb[oi % 2][:],
                                 in_=attnd[0][g * 128:(g + 1) * 128, hf * THF:(hf + 1) * THF])
                ic = q.dma_start(out=at2b[oi % 2][:],
                                 in_=attnd[1][g * 128:(g + 1) * 128, hf * THF:(hf + 1) * THF])
                s_ = outldS.s[outldS.i % outldS.k]
                s_.inc(ia, 16)
                s_.inc(ib, 16)
                s_.inc(ic, 16)
                outldS.i += 1
                oev[(oi, "ld")] = (s_, s_.n)

            @block.vector
            def _(q, j=j, oi=oi, cs=cs, acc_ready=acc_ready):
                wtok(q, oev[(oi, "ld")])
                q.wait_ge(sDV.h, acc_ready)
                q.wait_ge(sDV.h, sDV.n)
                sDV.inc(q.tensor_tensor(out=stg[oi % 2][:], in0=ACC[:, cs],
                                        in1=x2b[oi % 2][:], op=OP.add))
                q.wait_ge(sDV.h, sDV.n)
                sDV.inc(q.tensor_tensor(out=x2b[oi % 2][:], in0=stg[oi % 2][:],
                                        in1=atb[oi % 2][:], op=OP.add))
                q.wait_ge(sDV.h, sDV.n)
                ins = q.tensor_tensor(out=stg[oi % 2][:], in0=x2b[oi % 2][:],
                                      in1=at2b[oi % 2][:], op=OP.add)
                sDV.inc(ins)
                oev[(oi, "sum")] = sDV.n

            @block.sync
            def _(q, oi=oi, cs=cs):
                q.wait_ge(sDV.h, oev[(oi, "sum")])
                ins = q.dma_start(out=out_st[:, cs], in_=stg[oi % 2][:])
                oev[(oi, "st")] = outstS.put(ins)

        for th in range(2):
            bidx = th
            ps_h2x = contextlib.ExitStack()
            ps_h2 = [ps_h2x.enter_context(nc.psum_tensor(f"ps_h2_{th}_{i}", [128, THF], F32))
                     for i in range(2)]
            for ub in range(U // 4):
                @block.sync
                def _(q, bidx=bidx, th=th, ub=ub):
                    if bidx == 0 and ub == 0:
                        wtok(q, hid1_done)
                    pk = (bidx, 4 * (ub - 2) + 3, "h64") if ub >= 2 else (bidx - 1, 4 * (ub + 14) + 3, "h64")
                    if pk in f2:
                        q.wait_ge(sPE.h, f2[pk])
                    ins = q.dma_start(out=hp2[ub % 2][:],
                                      in_=hview[:, ub * 4:(ub + 1) * 4, th * THF:(th + 1) * THF])
                    f2[(bidx, ub, "hp")] = hpS.put(ins)

                for k in range(4):
                    u = ub * 4 + k
                    @block.tensor
                    def _(q, bidx=bidx, ub=ub, k=k, u=u):
                        if u == 0:
                            if bidx == 0:
                                q.wait_ge(sAC.h, sAC.n)
                                q.wait_ge(sDV.h, sDV.n)
                            else:
                                q.wait_ge(sPE.h, f2[(1, "w2mm")])
                                q.wait_ge(sDV.h, f2[(1, "accev")])
                        if k == 0:
                            wtok(q, f2[(bidx, ub, "hp")])
                        if (bidx, u - 2, "h2ev") in f2:
                            q.wait_ge(sDV.h, f2[(bidx, u - 2, "h2ev")])
                        ins = q.matmul(ps_h2[u % 2][:], hbd_s[:],
                                       hp2[ub % 2][:, k * THF:(k + 1) * THF],
                                       start=True, stop=True)
                        sPE.inc(ins)
                        f2[(bidx, u, "h64")] = sPE.n

                    @block.vector
                    def _(q, bidx=bidx, u=u):
                        q.wait_ge(sPE.h, f2[(bidx, u, "h64")])
                        ins = q.tensor_scalar(out=hid2[:, u * THF:(u + 1) * THF],
                                              in0=ps_h2[u % 2][:], scalar1=FSCALE,
                                              scalar2=None, op0=OP.mult)
                        sDV.inc(ins)
                        f2[(bidx, u, "h2ev")] = sDV.n
            hid2_done = sDV.n
            ps_h2x.close()

            for dh in range(2):
                wsub = th * 2 + dh
                ps_w2x = contextlib.ExitStack()
                ps_w2 = ps_w2x.enter_context(nc.psum_tensor(f"ps_w2_{wsub}", [128, 8 * THF], F32))
                for u in range(U):
                    idx = wsub * U + u
                    @block.gpsimd
                    def _(q, u=u, idx=idx, dh=dh):
                        if idx < 4:
                            wtok(q, hid1_done)
                        else:
                            q.wait_ge(sPE.h, f2[(idx - 4, "w2u")])
                        w2_ready[idx] = w2S.put(
                            q.dma_start(out=w2buf[idx % 4][:],
                                        in_=w2_st[u][:, dh * 1024:(dh + 1) * 1024]))

                    @block.tensor
                    def _(q, wsub=wsub, u=u, idx=idx, hid2_done=hid2_done):
                        if u == 0:
                            q.wait_ge(sDV.h, hid2_done)
                            if (wsub - 1, "accev") in f2:
                                q.wait_ge(sDV.h, f2[(wsub - 1, "accev")])
                        wtok(q, w2_ready[idx])
                        for dtile in range(8):
                            ins = q.matmul(ps_w2[:, dtile * THF:(dtile + 1) * THF],
                                           w2buf[idx % 4][:, dtile * 128:(dtile + 1) * 128],
                                           hid2[:, u * THF:(u + 1) * THF],
                                           start=(u == 0), stop=(u == U - 1))
                        sPE.inc(ins)
                        f2[(idx, "w2u")] = sPE.n
                f2[(wsub, "w2mm")] = sPE.n

                @block.vector
                def _(q, dh=dh, th=th, wsub=wsub):
                    q.wait_ge(sPE.h, f2[(wsub, "w2mm")])
                    for dtile in range(8):
                        jt = dh * 8 + dtile
                        ins = q.tensor_copy(
                            out=ACC[:, jt * T + th * THF: jt * T + (th + 1) * THF],
                            in_=ps_w2[:, dtile * THF:(dtile + 1) * THF])
                    sDV.inc(ins)
                    f2[(wsub, "accev")] = sDV.n
                ps_w2x.close()
                for dtile in range(8):
                    emit_out(2 * (dh * 8 + dtile) + th, f2[(wsub, "accev")])
        acc_done = sDV.n

        if "acc" in taps:
            tp = mktap("acc", G * T, F32)
            @block.sync
            def _(q, tp=tp):
                q.wait_ge(sDV.h, acc_done)
                tapS.put(q.dma_start(out=tp, in_=ACC[:]))

        @block.sync
        def _(q):
            q.wait_ge(sSTO.h, sSTO.n)
            for strm in (parkS, h1S, outstS, tapS):
                wtok(q, strm.totals())

        es_out.close()
        es_f2.close()

    return nc


def _host_layouts(inputs):
    f32 = lambda a: np.ascontiguousarray(np.asarray(a, dtype=np.float32))
    x = f32(inputs["L0_attn_x"])[0]
    wqT = f32(inputs["L0_attn_wq"]).T
    wkT = f32(inputs["L0_attn_wk"]).T
    wvT = f32(inputs["L0_attn_wv"]).T
    woT = f32(inputs["L0_attn_wo"]).T
    w1T = f32(inputs["L0_ffn_W1"]).T
    w3T = f32(inputs["L0_ffn_W3"]).T
    W2T = f32(inputs["L0_ffn_W2"]).T

    def stream_lhsT(wT):
        n = wT.shape[1] // 128
        return np.ascontiguousarray(
            wT.reshape(G, 128, n, 128).transpose(2, 1, 0, 3).reshape(n, 128, G * 128))

    consts = dict(
        identc=np.eye(128, dtype=ml_dtypes.bfloat16),
        h128c=_hadamard(128).astype(ml_dtypes.bfloat16),
        hbdc=np.kron(np.eye(2), _hadamard(64)).astype(ml_dtypes.bfloat16),
        maskc=np.ascontiguousarray(
            np.tile(np.triu(np.ones((64, 64), np.float32)), (1, 4))),
        onesbc=np.ones((128, 1), ml_dtypes.bfloat16),
        onespc=np.ones((1, 128), np.float32),
        onesnc=-np.ones((1, 128), np.float32),
    )
    col = lambda v: np.ascontiguousarray(f32(v).reshape(128, 1))
    base = dict(
        wq_st=stream_lhsT(wqT), wk_st=stream_lhsT(wkT),
        wv_st=np.ascontiguousarray(
            wvT.reshape(G, 128, 4, THF).transpose(2, 1, 0, 3).reshape(4, 128, G * THF)),
        wo_st=np.ascontiguousarray(
            woT.reshape(2, 8, 128, G, 128).transpose(0, 3, 2, 1, 4).reshape(2, G, 128, 8 * 128)),
        w1_st=stream_lhsT(w1T), w3_st=stream_lhsT(w3T),
        w2_st=np.ascontiguousarray(
            W2T.reshape(64, 64, 2, D).transpose(1, 2, 0, 3).reshape(U, 128, D)),
        fq1T=np.ascontiguousarray(f32(inputs["L0_attn_fq1_w"]).T),
        fq2T=np.ascontiguousarray(f32(inputs["L0_attn_fq2_w"]).T),
        fk1T=np.ascontiguousarray(f32(inputs["L0_attn_fk1_w"]).T),
        fk2T=np.ascontiguousarray(f32(inputs["L0_attn_fk2_w"]).T),
        fq1b=col(inputs["L0_attn_fq1_b"]), fq2b=col(inputs["L0_attn_fq2_b"]),
        fk1b=col(inputs["L0_attn_fk1_b"]), fk2b=col(inputs["L0_attn_fk2_b"]),
        normw=col(inputs["L0_attn_norm_w"]),
        **consts,
    )
    in_maps = []
    for c in range(NCORES):
        xc = np.ascontiguousarray(
            x[c * T:(c + 1) * T, :].T.reshape(G, 128, T).transpose(1, 0, 2).reshape(128, G * T))
        w = np.zeros((128, NCORES), np.float32)
        w[:, :c] = 1.0
        in_maps.append(dict(base, x_st=xc, wsel=w))
    return in_maps


def kernel(**inputs):
    in_maps = _host_layouts(inputs)
    nc = build()
    res = run_bass_kernel_spmd(nc, in_maps, core_ids=list(range(NCORES)))
    outs = []
    for c in range(NCORES):
        o = res.results[c]["out_st"]
        o = o.reshape(128, G, T).transpose(1, 0, 2).reshape(D, T).T
        outs.append(o)
    return np.concatenate(outs, axis=0)[None, :, :].astype(np.float32)


# version 11 (final)
# speedup vs baseline: 1.0529x; 1.0529x over previous
import contextlib
import numpy as np
import ml_dtypes

import concourse.bass as bass
import concourse.mybir as mybir
from concourse.bass_utils import run_bass_kernel_spmd
from concourse.alu_op_type import AluOpType as OP

F32 = mybir.dt.float32
BF16 = mybir.dt.bfloat16
AF = mybir.ActivationFunctionType

NCORES = 8
T = 1024
THF = 512
D = 2048
G = 16
H = 16
DK = 128
CH = 64
HG = 64
U = 64
EPS = 1e-5
QSCALE = float(DK) ** -0.5
FSCALE = float(1.0 / np.sqrt(8192.0))


def _hadamard(n):
    Hm = np.array([[1.0]])
    while Hm.shape[0] < n:
        Hm = np.block([[Hm, Hm], [Hm, -Hm]])
    return Hm


class Sem:
    def __init__(self, es, nc, name):
        self.h = es.enter_context(nc.semaphore(name))
        self.n = 0

    def inc(self, instr, k=1):
        instr.then_inc(self.h, k)
        self.n += k
        return self.n


class DStream:
    def __init__(self, es, nc, name, nslots):
        self.s = [Sem(es, nc, f"{name}_{i}") for i in range(nslots)]
        self.k = nslots
        self.i = 0

    def put(self, instr, inc=16):
        s = self.s[self.i % self.k]
        s.inc(instr, inc)
        self.i += 1
        return (s, s.n)

    def put2(self, instr_a, instr_b):
        s = self.s[self.i % self.k]
        s.inc(instr_a, 16)
        s.inc(instr_b, 16)
        self.i += 1
        return (s, s.n)

    def totals(self):
        return [(s, s.n) for s in self.s]


def wtok(q, tok):
    if isinstance(tok, list):
        for t in tok:
            q.wait_ge(t[0].h, t[1])
    else:
        q.wait_ge(tok[0].h, tok[1])


def build(taps=(), stop_after=None):
    nc = bass.Bass("TRN2", target_bir_lowering=False, debug=False)
    dt = nc.dram_tensor

    x_st = dt("x_st", [128, G * T], F32, kind="ExternalInput").ap()
    wq_st = dt("wq_st", [G, 128, D], F32, kind="ExternalInput").ap()
    wk_st = dt("wk_st", [G, 128, D], F32, kind="ExternalInput").ap()
    wv_st = dt("wv_st", [4, 128, G * THF], F32, kind="ExternalInput").ap()
    wo_st = dt("wo_st", [2, G, 128, 8 * 128], F32, kind="ExternalInput").ap()
    w1_st = dt("w1_st", [HG, 128, D], F32, kind="ExternalInput").ap()
    w3_st = dt("w3_st", [HG, 128, D], F32, kind="ExternalInput").ap()
    w2_st = dt("w2_st", [U, 128, D], F32, kind="ExternalInput").ap()
    fq1T = dt("fq1T", [128, 128], F32, kind="ExternalInput").ap()
    fq2T = dt("fq2T", [128, 128], F32, kind="ExternalInput").ap()
    fk1T = dt("fk1T", [128, 128], F32, kind="ExternalInput").ap()
    fk2T = dt("fk2T", [128, 128], F32, kind="ExternalInput").ap()
    fq1b = dt("fq1b", [128, 1], F32, kind="ExternalInput").ap()
    fq2b = dt("fq2b", [128, 1], F32, kind="ExternalInput").ap()
    fk1b = dt("fk1b", [128, 1], F32, kind="ExternalInput").ap()
    fk2b = dt("fk2b", [128, 1], F32, kind="ExternalInput").ap()
    normw = dt("normw", [128, 1], F32, kind="ExternalInput").ap()
    wsel = dt("wsel", [128, NCORES], F32, kind="ExternalInput").ap()
    identc = dt("identc", [128, 128], BF16, kind="ExternalInput").ap()
    h128c = dt("h128c", [128, 128], BF16, kind="ExternalInput").ap()
    hbdc = dt("hbdc", [128, 128], BF16, kind="ExternalInput").ap()
    maskc = dt("maskc", [64, 256], F32, kind="ExternalInput").ap()
    onesbc = dt("onesbc", [128, 1], BF16, kind="ExternalInput").ap()
    onespc = dt("onespc", [1, 128], F32, kind="ExternalInput").ap()
    onesnc = dt("onesnc", [1, 128], F32, kind="ExternalInput").ap()

    out_st = dt("out_st", [128, G * T], F32, kind="ExternalOutput").ap()

    hid1d = dt("hid1d", [HG * 128, T], BF16).ap()
    attnd = [dt(f"attnd{p}", [D, T], F32).ap() for p in range(2)]
    cc_in = dt("cc_in", [D, 128], F32).ap()
    cc_out = dt("cc_out", [NCORES * D, 128], F32, addr_space="Shared").ap()

    tap_outs = {}

    def mktap(name, width, dtype, rows=128):
        tap_outs[name] = dt("tap_" + name, [rows, width], dtype,
                            kind="ExternalOutput").ap()
        return tap_outs[name]

    es = contextlib.ExitStack()
    with es:
        sLD = Sem(es, nc, "sLD")
        sW = Sem(es, nc, "sW")
        wkS = DStream(es, nc, "wkS", 3)
        wvS = DStream(es, nc, "wvS", 4)
        wqS = DStream(es, nc, "wqS", 2)
        woS = DStream(es, nc, "woS", 4)
        w13S = DStream(es, nc, "w13S", 2)
        w2S = DStream(es, nc, "w2S", 6)
        hpS = DStream(es, nc, "hpS", 4)
        gS = DStream(es, nc, "gS", 2)
        parkS = DStream(es, nc, "parkS", 2)
        h1S = DStream(es, nc, "h1S", 2)
        outldS = DStream(es, nc, "outldS", 2)
        outstS = DStream(es, nc, "outstS", 2)
        tapS = DStream(es, nc, "tapS", 10)
        sPE = Sem(es, nc, "sPE")
        sDV = Sem(es, nc, "sDV")
        sAC = Sem(es, nc, "sAC")
        sCC = Sem(es, nc, "sCC")
        sSTO = Sem(es, nc, "sSTO")
        block = es.enter_context(nc.Block())

        sb = nc.sbuf_tensor
        xn = es.enter_context(sb("xn", [128, G * T], BF16))
        ident_s = es.enter_context(sb("ident_s", [128, 128], BF16))
        h128_s = es.enter_context(sb("h128_s", [128, 128], BF16))
        hbd_s = es.enter_context(sb("hbd_s", [128, 128], BF16))
        mask_s = es.enter_context(sb("mask_s", [64, 256], F32))
        onesb_s = es.enter_context(sb("onesb_s", [128, 1], BF16))
        onesp_s = es.enter_context(sb("onesp_s", [1, 128], F32))
        onesn_s = es.enter_context(sb("onesn_s", [1, 128], F32))
        fq1T_s = es.enter_context(sb("fq1T_s", [128, 128], BF16))
        fq2T_s = es.enter_context(sb("fq2T_s", [128, 128], BF16))
        fk1T_s = es.enter_context(sb("fk1T_s", [128, 128], BF16))
        fk2T_s = es.enter_context(sb("fk2T_s", [128, 128], BF16))
        fq1b_s = es.enter_context(sb("fq1b_s", [128, 1], F32))
        fq2b_s = es.enter_context(sb("fq2b_s", [128, 1], F32))
        fk1b_s = es.enter_context(sb("fk1b_s", [128, 1], F32))
        fk2b_s = es.enter_context(sb("fk2b_s", [128, 1], F32))
        normw_s = es.enter_context(sb("normw_s", [128, 1], F32))
        wsel_s = es.enter_context(sb("wsel_s", [128, NCORES], F32))

        @block.sync
        def _(q):
            for dst, src in [(ident_s, identc), (h128_s, h128c), (hbd_s, hbdc),
                             (mask_s, maskc), (onesb_s, onesbc), (onesp_s, onespc),
                             (onesn_s, onesnc), (fq1b_s, fq1b), (fq2b_s, fq2b),
                             (fk1b_s, fk1b), (fk2b_s, fk2b), (normw_s, normw),
                             (wsel_s, wsel)]:
                sLD.inc(q.dma_start(out=dst[:], in_=src), 16)

        @block.gpsimd
        def _(q):
            for dst, src in [(fq1T_s, fq1T), (fq2T_s, fq2T),
                             (fk1T_s, fk1T), (fk2T_s, fk2T)]:
                sW.inc(q.dma_start(out=dst[:], in_=src), 16)
        n_const = sLD.n
        n_fmapw = sW.n

        def finish_early():
            @block.sync
            def _(q):
                sSTO.inc(q.dma_start(out=out_st, in_=x_st), 16)
                q.wait_ge(sSTO.h, sSTO.n)
                for strm in (parkS, h1S, outstS, tapS):
                    wtok(q, strm.totals())

        es_ln = contextlib.ExitStack()
        x_s = es_ln.enter_context(sb("x_s", [128, G * T], F32))
        xb_r = [es_ln.enter_context(sb(f"xb{i}", [128, T], BF16)) for i in range(2)]
        sq_r = [es_ln.enter_context(sb(f"sq{i}", [128, T], BF16)) for i in range(2)]
        st1 = es_ln.enter_context(sb("st1", [1, 14 * THF], F32))
        xtmp = [es_ln.enter_context(sb(f"xtmp{i}", [128, THF], F32)) for i in range(4)]
        ps_ln = contextlib.ExitStack()
        ps_s1 = [ps_ln.enter_context(nc.psum_tensor(f"ps_s1{i}", [1, THF], F32)) for i in range(2)]
        ps_s2 = [ps_ln.enter_context(nc.psum_tensor(f"ps_s2{i}", [1, THF], F32)) for i in range(2)]
        ps_A = [ps_ln.enter_context(nc.psum_tensor(f"ps_A{i}", [128, THF], F32)) for i in range(2)]
        ps_B = [ps_ln.enter_context(nc.psum_tensor(f"ps_B{i}", [128, THF], F32)) for i in range(2)]

        @block.sync
        def _(q):
            sLD.inc(q.dma_start(out=x_s[:], in_=x_st), 16)
        n_xload = sLD.n

        ln_evt = {}
        for g in range(G):
            @block.vector
            def _(q, g=g):
                if g == 0:
                    q.wait_ge(sLD.h, n_xload)
                if g >= 2:
                    q.wait_ge(sPE.h, ln_evt[(g - 2, "mm")])
                ins = q.tensor_copy(out=xb_r[g % 2][:], in_=x_s[:, g * T:(g + 1) * T])
                sDV.inc(ins)
                ln_evt[(g, "cast")] = sDV.n

            @block.scalar
            def _(q, g=g):
                if g == 0:
                    q.wait_ge(sLD.h, n_xload)
                if g >= 2:
                    q.wait_ge(sPE.h, ln_evt[(g - 2, "mm")])
                ins = q.activation(out=sq_r[g % 2][:], in_=x_s[:, g * T:(g + 1) * T],
                                   func=AF.Square)
                sAC.inc(ins)
                ln_evt[(g, "sq")] = sAC.n

            @block.tensor
            def _(q, g=g):
                q.wait_ge(sDV.h, ln_evt[(g, "cast")])
                q.wait_ge(sAC.h, ln_evt[(g, "sq")])
                for th in range(2):
                    q.matmul(ps_s1[th][:], onesb_s[:], xb_r[g % 2][:, th * THF:(th + 1) * THF],
                             start=(g == 0), stop=(g == G - 1))
                for th in range(2):
                    ins = q.matmul(ps_s2[th][:], onesb_s[:], sq_r[g % 2][:, th * THF:(th + 1) * THF],
                                   start=(g == 0), stop=(g == G - 1))
                sPE.inc(ins)
                ln_evt[(g, "mm")] = sPE.n
        stats_done = sPE.n

        col = lambda i, th: st1[:, (7 * th + i) * THF:(7 * th + i + 1) * THF]
        ep_ready = []
        for th in range(2):
            mean_c, e2_c, msq_c, var_c, lnv_c, rst_c, mr_c = [col(i, th) for i in range(7)]

            @block.scalar
            def _(q, th=th, mean_c=mean_c, e2_c=e2_c, msq_c=msq_c):
                q.wait_ge(sPE.h, stats_done)
                sAC.inc(q.activation(out=mean_c, in_=ps_s1[th][:], func=AF.Copy, scale=1.0 / D))
                q.activation(out=e2_c, in_=ps_s2[th][:], func=AF.Copy, scale=1.0 / D)
                q.wait_ge(sAC.h, sAC.n)
                sAC.inc(q.activation(out=msq_c, in_=mean_c, func=AF.Square))
            n1 = sAC.n

            @block.vector
            def _(q, n1=n1, var_c=var_c, e2_c=e2_c, msq_c=msq_c):
                q.wait_ge(sAC.h, n1)
                sDV.inc(q.tensor_tensor(out=var_c, in0=e2_c, in1=msq_c, op=OP.subtract))
            n2 = sDV.n

            @block.scalar
            def _(q, n2=n2, lnv_c=lnv_c, var_c=var_c, rst_c=rst_c):
                q.wait_ge(sDV.h, n2)
                sAC.inc(q.activation(out=lnv_c, in_=var_c, func=AF.Ln, bias=np.float32(EPS)))
                q.wait_ge(sAC.h, sAC.n)
                sAC.inc(q.activation(out=rst_c, in_=lnv_c, func=AF.Exp, scale=-0.5))
            n3 = sAC.n

            @block.vector
            def _(q, n3=n3, mr_c=mr_c, mean_c=mean_c, rst_c=rst_c):
                q.wait_ge(sAC.h, n3)
                sDV.inc(q.tensor_tensor(out=mr_c, in0=mean_c, in1=rst_c, op=OP.mult))
            n4 = sDV.n

            @block.tensor
            def _(q, th=th, n3=n3, n4=n4, rst_c=rst_c, mr_c=mr_c):
                q.wait_ge(sAC.h, n3)
                q.matmul(ps_A[th][:], onesp_s[:], rst_c, start=True, stop=True)
                q.wait_ge(sDV.h, n4)
                sPE.inc(q.matmul(ps_B[th][:], onesn_s[:], mr_c, start=True, stop=True))
            ep_ready.append(sPE.n)

        xn_tile_done = {}
        @block.vector
        def _(q):
            mul_cnt = {}
            add_cnt = xn_tile_done
            for th in range(2):
                q.wait_ge(sPE.h, ep_ready[th])
                for g in range(G):
                    xs = x_s[:, g * T + th * THF: g * T + (th + 1) * THF]
                    if (th, g - 4) in add_cnt:
                        q.wait_ge(sDV.h, add_cnt[(th, g - 4)])
                    sDV.inc(q.tensor_tensor(out=xtmp[g % 4][:], in0=xs, in1=ps_A[th][:], op=OP.mult))
                    mul_cnt[(th, g)] = sDV.n
                    gp = g - 1
                    if gp >= 0:
                        q.wait_ge(sDV.h, mul_cnt[(th, gp)])
                        ins = q.tensor_tensor(out=xn[:, gp * T + th * THF: gp * T + (th + 1) * THF],
                                              in0=xtmp[gp % 4][:], in1=ps_B[th][:], op=OP.add)
                        sDV.inc(ins)
                        add_cnt[(th, gp)] = sDV.n
                q.wait_ge(sDV.h, mul_cnt[(th, G - 1)])
                ins = q.tensor_tensor(out=xn[:, (G - 1) * T + th * THF: (G - 1) * T + (th + 1) * THF],
                                      in0=xtmp[(G - 1) % 4][:], in1=ps_B[th][:], op=OP.add)
                sDV.inc(ins)
                add_cnt[(th, G - 1)] = sDV.n
        xn_done = sDV.n

        if "xn" in taps:
            tp = mktap("xn", G * T, BF16)
            @block.sync
            def _(q, tp=tp):
                q.wait_ge(sDV.h, xn_done)
                tapS.put(q.dma_start(out=tp, in_=xn[:]))

        es_ln.close()
        ps_ln.close()
        if stop_after == "ln":
            finish_early()
            return nc

        es_kv = contextlib.ExitStack()
        k_sb = es_kv.enter_context(sb("k_sb", [128, H * T], BF16))
        states = [es_kv.enter_context(sb(f"states{i}", [128, H * 128], F32)) for i in range(2)]
        kT_sb = es_kv.enter_context(sb("kT_sb", [128, H * 8 * 128], BF16))
        v_sb = es_kv.enter_context(sb("v_sb", [128, 8 * D], BF16))
        wqbuf = [es_kv.enter_context(sb(f"wqbuf{i}", [128, D], BF16)) for i in range(2)]
        wobuf = [es_kv.enter_context(sb(f"wobuf{i}", [128, 8 * 128], BF16)) for i in range(4)]
        es_v = contextlib.ExitStack()
        wvbuf = [es_v.enter_context(sb(f"wvbuf{i}", [128, 4 * THF], BF16)) for i in range(4)]
        es_kp = contextlib.ExitStack()
        wkbuf = [es_kp.enter_context(sb(f"wkbuf{i}", [128, D], BF16)) for i in range(3)]
        kpre = [es_kp.enter_context(sb(f"kpre{i}", [128, THF], BF16)) for i in range(2)]
        e1b = es_kp.enter_context(sb("e1b", [128, THF], F32))
        e2b = es_kp.enter_context(sb("e2b", [128, THF], F32))

        ps_kv1 = contextlib.ExitStack()
        ps_k = [ps_kv1.enter_context(nc.psum_tensor(f"ps_k{i}", [128, THF], F32)) for i in range(4)]
        ps_f1 = ps_kv1.enter_context(nc.psum_tensor("ps_f1", [128, THF], F32))
        ps_f2 = ps_kv1.enter_context(nc.psum_tensor("ps_f2", [128, THF], F32))
        ps_tr = [ps_kv1.enter_context(nc.psum_tensor(f"ps_tr{i}", [128, 128], BF16)) for i in range(2)]

        wk_ready = []
        kv_evt = {}

        def kv_back(jt):
            for th in range(2):
                @block.scalar
                def _(q, jt=jt, th=th):
                    q.wait_ge(sPE.h, kv_evt[(jt, th, "kmm")])
                    if (jt - 1, th, "fmm") in kv_evt:
                        q.wait_ge(sPE.h, kv_evt[(jt - 1, th, "fmm")])
                    ins = q.activation(out=kpre[th][:], in_=ps_k[2 * (jt % 2) + th][:], func=AF.Copy)
                    sAC.inc(ins)
                    kv_evt[(jt, th, "kpre_ac")] = sAC.n

                @block.tensor
                def _(q, jt=jt, th=th):
                    q.wait_ge(sAC.h, kv_evt[(jt, th, "kpre_ac")])
                    prev = kv_evt.get("fmap_dv")
                    if prev is not None:
                        q.wait_ge(sDV.h, prev)
                    q.matmul(ps_f1[:], fk1T_s[:], kpre[th][:], start=True, stop=True)
                    sPE.inc(q.matmul(ps_f2[:], fk2T_s[:], kpre[th][:], start=True, stop=True))
                    kv_evt[(jt, th, "fmm")] = sPE.n

                @block.vector
                def _(q, jt=jt, th=th):
                    q.wait_ge(sPE.h, kv_evt[(jt, th, "fmm")])
                    q.wait_ge(sDV.h, sDV.n)
                    q.tensor_scalar(out=e1b[:], in0=ps_f1[:], scalar1=fk1b_s[:],
                                    scalar2=None, op0=OP.add)
                    sDV.inc(q.tensor_scalar(out=e2b[:], in0=ps_f2[:], scalar1=fk2b_s[:],
                                            scalar2=None, op0=OP.add))
                    q.wait_ge(sDV.h, sDV.n)
                    ins = q.tensor_tensor(
                        out=k_sb[:, jt * T + th * THF: jt * T + (th + 1) * THF],
                        in0=e1b[:], in1=e2b[:], op=OP.mult)
                    sDV.inc(ins)
                    kv_evt["fmap_dv"] = sDV.n
                    kv_evt[(jt, th, "k")] = sDV.n

            for tt in range(8):
                @block.tensor
                def _(q, jt=jt, tt=tt):
                    if tt == 0:
                        q.wait_ge(sDV.h, kv_evt[(jt, 1, "k")])
                    pk = (jt, tt - 2, "trev") if tt >= 2 else (jt - 1, tt + 6, "trev")
                    if pk in kv_evt:
                        q.wait_ge(sDV.h, kv_evt[pk])
                    ins = q.transpose(ps_tr[tt % 2][:],
                                      k_sb[:, jt * T + tt * 128: jt * T + (tt + 1) * 128],
                                      ident_s[:])
                    sPE.inc(ins)
                    kv_evt[(jt, tt, "tr")] = sPE.n

                @block.vector
                def _(q, jt=jt, tt=tt):
                    q.wait_ge(sPE.h, kv_evt[(jt, tt, "tr")])
                    ins = q.tensor_copy(
                        out=kT_sb[:, (jt * 8 + tt) * 128:(jt * 8 + tt + 1) * 128],
                        in_=ps_tr[tt % 2][:])
                    sDV.inc(ins)
                    kv_evt[(jt, tt, "trev")] = sDV.n

        for jt in range(G):
            @block.gpsimd
            def _(q, jt=jt):
                if jt < 3:
                    q.wait_ge(sDV.h, xn_done)
                else:
                    q.wait_ge(sPE.h, kv_evt[(jt - 3, 1, "kmm")])
                wk_ready.append(wkS.put(q.dma_start(out=wkbuf[jt % 3][:], in_=wk_st[jt])))

            @block.tensor
            def _(q, jt=jt):
                wtok(q, wk_ready[jt])
                if jt == 0:
                    q.wait_ge(sW.h, n_fmapw)
                for th in range(2):
                    if (jt - 2, th, "kpre_ac") in kv_evt:
                        q.wait_ge(sAC.h, kv_evt[(jt - 2, th, "kpre_ac")])
                    for g in range(G):
                        if jt == 0:
                            q.wait_ge(sDV.h, xn_tile_done[(th, g)])
                        ins = q.matmul(ps_k[2 * (jt % 2) + th][:], wkbuf[jt % 3][:, g * 128:(g + 1) * 128],
                                       xn[:, g * T + th * THF: g * T + (th + 1) * THF],
                                       start=(g == 0), stop=(g == G - 1))
                    sPE.inc(ins)
                    kv_evt[(jt, th, "kmm")] = sPE.n

            if jt - 1 >= 0:
                kv_back(jt - 1)
        kv_back(G - 1)
        k_done = sDV.n
        kT_done = sDV.n
        kmm_all = sPE.n
        ps_kv1.close()
        es_kp.close()

        ps_v = contextlib.ExitStack()
        ps_vt = [ps_v.enter_context(nc.psum_tensor(f"ps_v{t}", [128, THF], F32)) for t in range(8)]

        wv_ready = []
        for jb in range(4):
            for gq in range(4):
                i = jb * 4 + gq
                @block.gpsimd
                def _(q, jb=jb, gq=gq, i=i):
                    if i < 4:
                        q.wait_ge(sDV.h, xn_done)
                    else:
                        pi = divmod(4 * (i - 4) + 3, G)
                        q.wait_ge(sPE.h, kv_evt[(pi[0], pi[1], "vmm")])
                    ins = q.dma_start(out=wvbuf[i % 4][:],
                                      in_=wv_st[jb][:, gq * 4 * THF:(gq + 1) * 4 * THF])
                    wv_ready.append(wvS.put(ins))

                for gg in range(4):
                    g = gq * 4 + gg
                    @block.tensor
                    def _(q, jb=jb, gq=gq, gg=gg, g=g, i=i):
                        if jb == 0 and g == 0:
                            q.wait_ge(sAC.h, sAC.n)
                            q.wait_ge(sDV.h, sDV.n)
                        if g == 0 and jb > 0:
                            q.wait_ge(sAC.h, kv_evt[(jb - 1, "vev")])
                        if gg == 0:
                            wtok(q, wv_ready[i])
                        for tt in range(8):
                            ins = q.matmul(ps_vt[tt][:],
                                           xn[:, g * T + tt * 128: g * T + (tt + 1) * 128],
                                           wvbuf[i % 4][:, gg * THF:(gg + 1) * THF],
                                           start=(g == 0), stop=(g == G - 1))
                        sPE.inc(ins)
                        kv_evt[(jb, g, "vmm")] = sPE.n

            @block.scalar
            def _(q, jb=jb):
                q.wait_ge(sPE.h, kv_evt[(jb, G - 1, "vmm")])
                for tt in range(8):
                    ins = q.activation(out=v_sb[:, tt * D + jb * THF: tt * D + (jb + 1) * THF],
                                       in_=ps_vt[tt][:], func=AF.Copy)
                sAC.inc(ins)
                kv_evt[(jb, "vev")] = sAC.n
        v_done = sAC.n
        ps_v.close()
        es_v.close()
        es_q2 = contextlib.ExitStack()
        q_sb = es_q2.enter_context(sb("q_sb", [128, H * THF], BF16))
        qpre = [es_q2.enter_context(sb(f"qpre{i}", [128, THF], BF16)) for i in range(2)]
        qe1 = es_q2.enter_context(sb("qe1", [128, THF], F32))
        qe2 = es_q2.enter_context(sb("qe2", [128, THF], F32))

        es_S = contextlib.ExitStack()
        Ssb = es_S.enter_context(sb("Ssb", [128, H * 128], F32))
        ps_Sx = contextlib.ExitStack()
        ps_S = [ps_Sx.enter_context(nc.psum_tensor(f"ps_S{i}", [128, 128], F32)) for i in range(2)]
        for h in range(H):
            @block.tensor
            def _(q, h=h):
                if h == 0:
                    q.wait_ge(sDV.h, kT_done)
                    q.wait_ge(sAC.h, v_done)
                if (h - 2, "Sev") in kv_evt:
                    q.wait_ge(sDV.h, kv_evt[(h - 2, "Sev")])
                for tt in range(8):
                    ins = q.matmul(ps_S[h % 2][:],
                                   kT_sb[:, (h * 8 + tt) * 128:(h * 8 + tt + 1) * 128],
                                   v_sb[:, tt * D + h * 128: tt * D + (h + 1) * 128],
                                   start=(tt == 0), stop=(tt == 7))
                sPE.inc(ins)
                kv_evt[(h, "Smm")] = sPE.n

            @block.vector
            def _(q, h=h):
                q.wait_ge(sPE.h, kv_evt[(h, "Smm")])
                ins = q.tensor_copy(out=Ssb[:, h * 128:(h + 1) * 128], in_=ps_S[h % 2][:])
                sDV.inc(ins)
                kv_evt[(h, "Sev")] = sDV.n
        S_done = sDV.n
        ps_Sx.close()

        @block.sync
        def _(q):
            q.wait_ge(sDV.h, S_done)
            sSTO.inc(q.dma_start(out=cc_in.rearrange("(h d) e -> d h e", h=H), in_=Ssb[:]), 16)
        ccin_done = sSTO.n

        wq_ready = {}
        @block.gpsimd
        def _(q):
            for jt in range(2):
                wq_ready[(0, jt)] = wqS.put(q.dma_start(out=wqbuf[jt % 2][:], in_=wq_st[jt]))
            q.wait_ge(sSTO.h, ccin_done)
            sCC.inc(q.collective_compute(
                "AllGather", OP.bypass,
                ins=[cc_in.opt()], outs=[cc_out.opt()],
                replica_groups=[list(range(NCORES))],
            ))
        cc_done = sCC.n

        gbuf = [es_S.enter_context(sb(f"gbuf{i}", [128, H * 64], F32)) for i in range(2)]
        gview = cc_out.rearrange("(r hf hd d) e -> r hf d hd e", r=NCORES, hf=2, d=128)
        pfx_evt = {}
        HW = H * 64
        for r in range(NCORES):
            for hf in range(2):
                i = 2 * r + hf
                @block.sync
                def _(q, r=r, hf=hf, i=i):
                    if i == 0:
                        q.wait_ge(sCC.h, cc_done)
                    if (i - 2, "pfx") in pfx_evt:
                        q.wait_ge(sDV.h, pfx_evt[(i - 2, "pfx")])
                    pfx_evt[(i, "ld")] = gS.put(q.dma_start(out=gbuf[i % 2][:], in_=gview[r, hf]))

                @block.vector
                def _(q, r=r, hf=hf, i=i):
                    wtok(q, pfx_evt[(i, "ld")])
                    q.wait_ge(sDV.h, sDV.n)
                    csl = slice(hf * HW, (hf + 1) * HW)
                    if r == 0:
                        ins = q.tensor_scalar(out=states[1][:, csl], in0=gbuf[i % 2][:],
                                              scalar1=wsel_s[:, 0:1], scalar2=None, op0=OP.mult)
                    else:
                        sDV.inc(q.tensor_scalar(out=gbuf[i % 2][:], in0=gbuf[i % 2][:],
                                                 scalar1=wsel_s[:, r:r + 1], scalar2=None, op0=OP.mult))
                        q.wait_ge(sDV.h, sDV.n)
                        s_, t_ = (states[1], states[0]) if r % 2 == 1 else (states[0], states[1])
                        ins = q.tensor_tensor(out=t_[:, csl], in0=s_[:, csl],
                                              in1=gbuf[i % 2][:], op=OP.add)
                    sDV.inc(ins)
                    pfx_evt[(i, "pfx")] = sDV.n
        pfx_done = sDV.n
        es_S.close()

        guard_pre_scan = []
        for nm, buf, wid, dty, wsem, wn in [
                ("k", k_sb, H * T, BF16, sDV, k_done),
                ("v", v_sb, 8 * D, BF16, sAC, v_done),
                ("kT", kT_sb, H * 8 * 128, BF16, sDV, kT_done),
                ("states", states[0], H * 128, F32, sDV, pfx_done)]:
            if nm in taps:
                tp = mktap(nm, wid, dty)
                @block.sync
                def _(q, tp=tp, buf=buf, wsem=wsem, wn=wn):
                    q.wait_ge(wsem.h, wn)
                    guard_pre_scan.append(tapS.put(q.dma_start(out=tp, in_=buf[:])))

        if stop_after == "kv":
            finish_early()
            es_kv.close()
            return nc

        es_qs = contextlib.ExitStack()
        sTm_sb = es_qs.enter_context(sb("sTm_sb", [128, 256], BF16))
        stb = [es_qs.enter_context(sb(f"stb{i}", [128, 512], BF16)) for i in range(2)]
        o_bf = es_qs.enter_context(sb("o_bf", [128, 4 * THF], BF16))
        o_grp = es_qs.enter_context(sb("o_grp", [128, 8 * THF], BF16))
        sqo = [es_qs.enter_context(sb("sqo0", [128, THF], BF16))] * 2
        lnr = es_qs.enter_context(sb("lnr", [1, THF], F32))
        park = [es_qs.enter_context(sb(f"park{i}", [128, THF], F32)) for i in range(2)]

        sc = {}
        wo_ready = {}
        guard_q0 = []

        def prevchunk(t2, grp, n, k):
            n2 = n - k
            if n2 >= 0:
                return (t2, grp, n2)
            if grp > 0:
                return (t2, grp - 1, n2 + 8)
            if t2 > 0:
                return (t2 - 1, 3, n2 + 8)
            return None

        for t2 in range(2):
            ps_q = contextlib.ExitStack()
            ps_qp = [ps_q.enter_context(nc.psum_tensor(f"ps_qp{t2}_{i}", [128, THF], F32)) for i in range(2)]
            ps_qf1 = ps_q.enter_context(nc.psum_tensor(f"ps_qf1{t2}", [128, THF], F32))
            ps_qf2 = ps_q.enter_context(nc.psum_tensor(f"ps_qf2{t2}", [128, THF], F32))
            def q_back(jt, t2=t2):
                @block.scalar
                def _(q, t2=t2, jt=jt):
                    q.wait_ge(sPE.h, sc[(t2, jt, "qmm")])
                    pk = (t2, jt - 2, "qfmm") if jt >= 2 else (t2 - 1, G + jt - 2, "qfmm")
                    if pk in sc:
                        q.wait_ge(sPE.h, sc[pk])
                    sAC.inc(q.activation(out=qpre[jt % 2][:], in_=ps_qp[jt % 2][:], func=AF.Copy))
                    sc[(t2, jt, "qpre_ac")] = sAC.n

                @block.tensor
                def _(q, t2=t2, jt=jt):
                    q.wait_ge(sAC.h, sc[(t2, jt, "qpre_ac")])
                    prev = sc.get("qfmap_dv")
                    if prev is not None:
                        q.wait_ge(sDV.h, prev)
                    q.matmul(ps_qf1[:], fq1T_s[:], qpre[jt % 2][:], start=True, stop=True)
                    sPE.inc(q.matmul(ps_qf2[:], fq2T_s[:], qpre[jt % 2][:], start=True, stop=True))
                    sc[(t2, jt, "qfmm")] = sPE.n

                @block.vector
                def _(q, t2=t2, jt=jt):
                    q.wait_ge(sPE.h, sc[(t2, jt, "qfmm")])
                    q.wait_ge(sDV.h, sDV.n)
                    if t2 == 1:
                        q.wait_ge(sPE.h, sc[(0, jt // 4, 7, "omm")])
                        q.wait_ge(sAC.h, sc[(0, 3, 3, "ln")])
                        if guard_q0:
                            wtok(q, guard_q0)
                    q.tensor_scalar(out=qe1[:], in0=ps_qf1[:], scalar1=fq1b_s[:],
                                    scalar2=QSCALE, op0=OP.add, op1=OP.mult)
                    sDV.inc(q.tensor_scalar(out=qe2[:], in0=ps_qf2[:], scalar1=fq2b_s[:],
                                            scalar2=None, op0=OP.add))
                    q.wait_ge(sDV.h, sDV.n)
                    ins = q.tensor_tensor(out=q_sb[:, jt * THF:(jt + 1) * THF],
                                          in0=qe1[:], in1=qe2[:], op=OP.mult)
                    sDV.inc(ins)
                    sc["qfmap_dv"] = sDV.n
                    sc[(t2, jt, "q")] = sDV.n

            for jt in range(G):
                if not (t2 == 0 and jt < 2):
                    @block.gpsimd
                    def _(q, t2=t2, jt=jt):
                        pk = (t2, jt - 2, "qmm") if jt >= 2 else (t2 - 1, G + jt - 2, "qmm")
                        q.wait_ge(sPE.h, sc[pk])
                        wq_ready[(t2, jt)] = wqS.put(q.dma_start(out=wqbuf[jt % 2][:], in_=wq_st[jt]))

                @block.tensor
                def _(q, t2=t2, jt=jt):
                    wtok(q, wq_ready[(t2, jt)])
                    if jt == 0 and t2 == 0:
                        q.wait_ge(sAC.h, v_done)
                        q.wait_ge(sDV.h, S_done)
                    pk = (t2, jt - 2, "qpre_ac") if jt >= 2 else (t2 - 1, G + jt - 2, "qpre_ac")
                    if pk in sc:
                        q.wait_ge(sAC.h, sc[pk])
                    for g in range(G):
                        ins = q.matmul(ps_qp[jt % 2][:], wqbuf[jt % 2][:, g * 128:(g + 1) * 128],
                                       xn[:, g * T + t2 * THF: g * T + (t2 + 1) * THF],
                                       start=(g == 0), stop=(g == G - 1))
                    sPE.inc(ins)
                    sc[(t2, jt, "qmm")] = sPE.n

                if jt - 1 >= 0:
                    q_back(jt - 1)
            q_back(G - 1)
            q_half_done = sDV.n
            ps_q.close()

            if f"q{t2}" in taps:
                tp = mktap(f"q{t2}", H * THF, BF16)
                @block.sync
                def _(q, tp=tp, n=q_half_done):
                    q.wait_ge(sDV.h, n)
                    tok = tapS.put(q.dma_start(out=tp, in_=q_sb[:]))
                    if t2 == 0:
                        guard_q0.append(tok)

            ps_sc = contextlib.ExitStack()
            ps_sT = [ps_sc.enter_context(nc.psum_tensor(f"ps_sT{t2}_{i}", [128, 256], F32)) for i in range(2)]
            ps_o = [ps_sc.enter_context(nc.psum_tensor(f"ps_o{t2}_{i}", [128, THF], F32)) for i in range(2)]
            ps_kvo = [ps_sc.enter_context(nc.psum_tensor(f"ps_kv{t2}_{i}", [128, 512], F32)) for i in range(2)]
            ps_rb = [ps_sc.enter_context(nc.psum_tensor(f"ps_rb{t2}_{i}", [128, THF], F32))
                     for i in range(2)]

            for grp in range(4):
                hs = [4 * grp + i for i in range(4)]
                gc = slice(grp * 512, (grp + 1) * 512)

                @block.vector
                def _(q, t2=t2, grp=grp, gc=gc):
                    if t2 == 0 and grp == 0:
                        q.wait_ge(sDV.h, pfx_done)
                        if guard_pre_scan:
                            wtok(q, guard_pre_scan)
                    if t2 == 1:
                        q.wait_ge(sDV.h, sc[(0, grp, 7, "caa")])
                    pg = (t2, grp - 1) if grp > 0 else ((t2 - 1, 3) if t2 > 0 else None)
                    if pg is not None:
                        q.wait_ge(sPE.h, sc[(pg[0], pg[1], 7, "omm")])
                    ins = q.tensor_copy(out=stb[0][:], in_=states[0][:, gc])
                    sDV.inc(ins)
                    sc[(t2, grp, "init")] = sDV.n

                def chunk_back(n, t2=t2, grp=grp, hs=hs, gc=gc):
                    m = t2 * 8 + n
                    par = m % 2
                    base = (m * CH) % 128
                    tt_idx = m // 2
                    last = (m == 15)

                    @block.tensor
                    def _(q, t2=t2, grp=grp, n=n, par=par, base=base, tt_idx=tt_idx,
                          hs=hs, last=last):
                        if grp % 2 == 0 and n < 2:
                            ppk = (t2, grp // 2 - 1, "park_ac_last") if grp >= 2 else (t2 - 1, 1, "park_ac_last")
                            if ppk in sc:
                                q.wait_ge(sAC.h, sc[ppk])
                        pk = prevchunk(t2, grp, n, 2)
                        if pk is not None and (pk + ("oev",)) in sc:
                            q.wait_ge(sAC.h, sc[pk + ("oev",)])
                        ck = sc[(t2, grp, "init")] if n == 0 else sc[(t2, grp, n - 1, "caa")]
                        q.wait_ge(sDV.h, ck)
                        q.wait_ge(sDV.h, sc[(t2, grp, n, "mask")])
                        for i, h in enumerate(hs):
                            q.matmul(ps_o[par][:, i * 64:(i + 1) * 64],
                                     stb[par][:, i * 128:(i + 1) * 128],
                                     q_sb[:, h * THF + n * CH: h * THF + (n + 1) * CH],
                                     start=True, stop=False)
                            ins = q.matmul(ps_o[par][:, i * 64:(i + 1) * 64],
                                           v_sb[base:base + 64,
                                                tt_idx * D + h * 128: tt_idx * D + (h + 1) * 128],
                                           sTm_sb[base:base + 64, i * 64:(i + 1) * 64],
                                           start=False, stop=True)
                        sPE.inc(ins)
                        sc[(t2, grp, n, "omm")] = sPE.n

                    @block.scalar
                    def _(q, t2=t2, grp=grp, n=n, par=par):
                        q.wait_ge(sPE.h, sc[(t2, grp, n, "omm")])
                        if n == 0:
                            pg = (t2, grp - 1) if grp > 0 else ((t2 - 1, 3) if t2 > 0 else None)
                            if pg is not None:
                                q.wait_ge(sDV.h, sc[(pg[0], pg[1], 3, "own")])
                        for i in range(4):
                            ins = q.activation(out=o_bf[:, i * THF + n * CH: i * THF + (n + 1) * CH],
                                               in_=ps_o[par][:, i * 64:(i + 1) * 64],
                                               func=AF.Copy)
                        sAC.inc(ins)
                        sc[(t2, grp, n, "oev")] = sAC.n

                    if not last:
                        @block.vector
                        def _(q, t2=t2, grp=grp, n=n, par=par, gc=gc):
                            q.wait_ge(sDV.h, sc[(t2, grp, n, "add")])
                            q.wait_ge(sPE.h, sc[(t2, grp, n, "omm")])
                            ins = q.tensor_copy(out=stb[1 - par][:], in_=states[1 - par][:, gc])
                            sDV.inc(ins)
                            sc[(t2, grp, n, "caa")] = sDV.n
                    else:
                        sc[(t2, grp, n, "caa")] = sc[(t2, grp, n - 1, "caa")]

                for n in range(8):
                    m = t2 * 8 + n
                    par = m % 2
                    base = (m * CH) % 128
                    tt_idx = m // 2
                    last = (m == 15)

                    @block.tensor
                    def _(q, t2=t2, grp=grp, n=n, par=par, base=base, tt_idx=tt_idx,
                          hs=hs, last=last):
                        if n == 0:
                            q.wait_ge(sDV.h, q_half_done)
                            q.wait_ge(sAC.h, sc.get((t2, G - 1, "qpre_ac"), 0))
                            if t2 == 0 and grp == 0:
                                q.wait_ge(sDV.h, pfx_done)
                                q.wait_ge(sDV.h, k_done)
                                q.wait_ge(sAC.h, v_done)
                        if not last:
                            pk = prevchunk(t2, grp, n, 2)
                            if pk is not None and (pk + ("add",)) in sc:
                                q.wait_ge(sDV.h, sc[pk + ("add",)])
                            for i, h in enumerate(hs):
                                ins = q.matmul(
                                    ps_kvo[par][:, i * 128:(i + 1) * 128],
                                    kT_sb[base:base + 64,
                                          (h * 8 + tt_idx) * 128:(h * 8 + tt_idx + 1) * 128],
                                    v_sb[base:base + 64,
                                         tt_idx * D + h * 128: tt_idx * D + (h + 1) * 128],
                                    start=True, stop=True)
                            sPE.inc(ins)
                            sc[(t2, grp, n, "kvmm")] = sPE.n
                        pk = prevchunk(t2, grp, n, 2)
                        if pk is not None and (pk + ("mask",)) in sc:
                            q.wait_ge(sDV.h, sc[pk + ("mask",)])
                        for i, h in enumerate(hs):
                            ins = q.matmul(
                                ps_sT[par][base:base + 64, i * 64:(i + 1) * 64],
                                k_sb[:, h * T + t2 * THF + n * CH: h * T + t2 * THF + (n + 1) * CH],
                                q_sb[:, h * THF + n * CH: h * THF + (n + 1) * CH],
                                start=True, stop=True)
                        sPE.inc(ins)
                        sc[(t2, grp, n, "sT")] = sPE.n

                    if not last:
                        @block.vector
                        def _(q, t2=t2, grp=grp, n=n, par=par, gc=gc):
                            q.wait_ge(sPE.h, sc[(t2, grp, n, "kvmm")])
                            q.wait_ge(sDV.h, sDV.n)
                            sDV.inc(q.tensor_tensor(out=states[1 - par][:, gc],
                                                    in0=states[par][:, gc],
                                                    in1=ps_kvo[par][:], op=OP.add))
                            sc[(t2, grp, n, "add")] = sDV.n

                    @block.vector
                    def _(q, t2=t2, grp=grp, n=n, par=par, base=base):
                        q.wait_ge(sPE.h, sc[(t2, grp, n, "sT")])
                        pk = prevchunk(t2, grp, n, 2)
                        if pk is not None and (pk + ("omm",)) in sc:
                            q.wait_ge(sPE.h, sc[pk + ("omm",)])
                        ins = q.tensor_tensor(out=sTm_sb[base:base + 64, :],
                                              in0=ps_sT[par][base:base + 64, :],
                                              in1=mask_s[:], op=OP.mult)
                        sDV.inc(ins)
                        sc[(t2, grp, n, "mask")] = sDV.n

                    if n >= 1:
                        chunk_back(n - 1)

                chunk_back(7)

                def rms_sq(i, t2=t2, grp=grp):
                    @block.scalar
                    def _(q, t2=t2, grp=grp, i=i):
                        if i == 0:
                            q.wait_ge(sAC.h, sc[(t2, grp, 7, "oev")])
                        pk = (t2, grp, i - 1, "ssmm")
                        if pk in sc:
                            q.wait_ge(sPE.h, sc[pk])
                        ins = q.activation(out=sqo[i % 2][:], in_=o_bf[:, i * THF:(i + 1) * THF],
                                           func=AF.Square)
                        sAC.inc(ins)
                        sc[(t2, grp, i, "sq")] = sAC.n

                def rms_ssmm(i, t2=t2, grp=grp):
                    @block.tensor
                    def _(q, t2=t2, grp=grp, i=i):
                        q.wait_ge(sAC.h, sc[(t2, grp, i, "sq")])
                        pk = (t2, grp, i - 2, "own")
                        if pk in sc:
                            q.wait_ge(sDV.h, sc[pk])
                        elif i < 2 and (t2, grp - 1, 2 + i, "own") in sc:
                            q.wait_ge(sDV.h, sc[(t2, grp - 1, 2 + i, "own")])
                        ins = q.matmul(ps_rb[i % 2][0:1, :], onesb_s[:], sqo[i % 2][:],
                                       start=True, stop=True)
                        sPE.inc(ins)
                        sc[(t2, grp, i, "ssmm")] = sPE.n

                def rms_ln(i, t2=t2, grp=grp):
                    @block.scalar
                    def _(q, t2=t2, grp=grp, i=i):
                        q.wait_ge(sPE.h, sc[(t2, grp, i, "ssmm")])
                        if (t2, grp, i - 1, "rbmm") in sc:
                            q.wait_ge(sPE.h, sc[(t2, grp, i - 1, "rbmm")])
                        sAC.inc(q.activation(out=qe2[0:1, :], in_=ps_rb[i % 2][0:1, :], func=AF.Ln,
                                             scale=1.0 / DK, bias=np.float32(EPS)))
                        q.wait_ge(sAC.h, sAC.n)
                        ins = q.activation(out=lnr[:], in_=qe2[0:1, :],
                                           func=AF.Exp, scale=-0.5)
                        sAC.inc(ins)
                        sc[(t2, grp, i, "ln")] = sAC.n

                def rms_rbmm(i, t2=t2, grp=grp):
                    @block.tensor
                    def _(q, t2=t2, grp=grp, i=i):
                        q.wait_ge(sAC.h, sc[(t2, grp, i, "ln")])
                        ins = q.matmul(ps_rb[i % 2][:], onesp_s[:], lnr[:],
                                       start=True, stop=True)
                        sPE.inc(ins)
                        sc[(t2, grp, i, "rbmm")] = sPE.n

                def rms_own(i, t2=t2, grp=grp):
                    slot = (grp % 2) * 4 + i
                    @block.vector
                    def _(q, t2=t2, grp=grp, i=i, slot=slot):
                        q.wait_ge(sPE.h, sc[(t2, grp, i, "rbmm")])
                        q.wait_ge(sDV.h, sDV.n)
                        pp = (t2, grp // 2 - 1, "wo_mm_last") if grp >= 2 else (t2 - 1, 1, "wo_mm_last")
                        if pp in sc:
                            q.wait_ge(sPE.h, sc[pp])
                        sDV.inc(q.tensor_scalar(out=qe1[:], in0=o_bf[:, i * THF:(i + 1) * THF],
                                                 scalar1=normw_s[:], scalar2=None, op0=OP.mult))
                        q.wait_ge(sDV.h, sDV.n)
                        ins = q.tensor_tensor(out=o_grp[:, slot * THF:(slot + 1) * THF],
                                              in0=qe1[:], in1=ps_rb[i % 2][:], op=OP.mult)
                        sDV.inc(ins)
                        sc[(t2, grp, i, "own")] = sDV.n

                rms_sq(0)
                rms_ssmm(0)
                rms_sq(1)
                rms_ln(0)
                rms_ssmm(1)
                rms_rbmm(0)
                rms_sq(2)
                rms_ln(1)
                rms_own(0)
                rms_ssmm(2)
                rms_rbmm(1)
                rms_sq(3)
                rms_ln(2)
                rms_own(1)
                rms_ssmm(3)
                rms_rbmm(2)
                rms_ln(3)
                rms_own(2)
                rms_rbmm(3)
                rms_own(3)

                if grp % 2 == 1:
                    p = grp // 2
                    hs8 = list(range(8 * p, 8 * p + 8))
                    for jt in range(G):
                        w = (t2 * 2 + p) * G + jt
                        @block.gpsimd
                        def _(q, t2=t2, p=p, jt=jt, w=w):
                            if w >= 4:
                                q.wait_ge(sPE.h, sc[("wo_mm", w - 4)])
                            wo_ready[w] = woS.put(q.dma_start(out=wobuf[w % 4][:], in_=wo_st[p, jt]))

                        @block.tensor
                        def _(q, t2=t2, grp=grp, p=p, jt=jt, w=w, hs8=hs8):
                            wtok(q, wo_ready[w])
                            if jt == 0:
                                q.wait_ge(sDV.h, sc[(t2, grp, 3, "own")])
                                q.wait_ge(sAC.h, sc[(t2, grp, 7, "oev")])
                            if jt >= 2:
                                q.wait_ge(sAC.h, sc[(t2, p, jt - 2, "park_ac")])
                            for s in range(8):
                                ins = q.matmul(ps_o[jt % 2][:],
                                               wobuf[w % 4][:, s * 128:(s + 1) * 128],
                                               o_grp[:, s * THF:(s + 1) * THF],
                                               start=(s == 0), stop=(s == 7))
                            sPE.inc(ins)
                            sc[("wo_mm", w)] = sPE.n
                            sc[(t2, p, jt, "wo_mm")] = sPE.n
                            if jt == G - 1:
                                sc[(t2, p, "wo_mm_last")] = sPE.n
                                if p == 1:
                                    sc[(t2, G - 1, "womm")] = sPE.n

                        @block.scalar
                        def _(q, t2=t2, p=p, jt=jt):
                            q.wait_ge(sPE.h, sc[(t2, p, jt, "wo_mm")])
                            pkst = (t2, p, jt - 2, "park_st") if jt >= 2 else \
                                   ((t2, p - 1, jt + 14, "park_st") if p >= 1 else (t2 - 1, 1, jt + 14, "park_st"))
                            if pkst in sc:
                                wtok(q, sc[pkst])
                            ins = q.activation(out=park[jt % 2][:], in_=ps_o[jt % 2][:], func=AF.Copy)
                            sAC.inc(ins)
                            sc[(t2, p, jt, "park_ac")] = sAC.n
                            if jt == G - 1:
                                sc[(t2, p, "park_ac_last")] = sAC.n

                        @block.sync
                        def _(q, t2=t2, p=p, jt=jt):
                            q.wait_ge(sAC.h, sc[(t2, p, jt, "park_ac")])
                            ins = q.dma_start(
                                out=attnd[p][jt * 128:(jt + 1) * 128, t2 * THF:(t2 + 1) * THF],
                                in_=park[jt % 2][:])
                            sc[(t2, p, jt, "park_st")] = parkS.put(ins)
            owo_done = sDV.n
            ps_sc.close()
        attn_parked = parkS.totals()
        es_qs.close()
        es_q2.close()
        es_kv.close()

        if "attn" in taps:
            tp0 = mktap("attn", T, F32, rows=D)
            tp1 = mktap("attn1", T, F32, rows=D)
            @block.sync
            def _(q, tp0=tp0, tp1=tp1):
                wtok(q, attn_parked)
                tapS.put(q.dma_start(out=tp0, in_=attnd[0]))
                tapS.put(q.dma_start(out=tp1, in_=attnd[1]))

        if stop_after == "attn":
            finish_early()
            return nc

        es_f1 = contextlib.ExitStack()
        w1buf = [es_f1.enter_context(sb(f"w1buf{i}", [128, D], BF16)) for i in range(2)]
        w3buf = [es_f1.enter_context(sb(f"w3buf{i}", [128, D], BF16)) for i in range(2)]
        silb = [es_f1.enter_context(sb(f"silb{i}", [128, THF], F32)) for i in range(2)]
        hidb = [es_f1.enter_context(sb(f"hidb{i}", [128, THF], BF16)) for i in range(2)]
        h1b = [es_f1.enter_context(sb(f"h1b{i}", [128, THF], BF16)) for i in range(2)]

        ps_f1s = contextlib.ExitStack()
        ps_g = [ps_f1s.enter_context(nc.psum_tensor(f"ps_g{i}", [128, THF], F32)) for i in range(2)]
        ps_l = [ps_f1s.enter_context(nc.psum_tensor(f"ps_l{i}", [128, THF], F32)) for i in range(2)]
        ps_h1 = [ps_f1s.enter_context(nc.psum_tensor(f"ps_h1{i}", [128, THF], F32)) for i in range(2)]

        w13_ready = {}
        f1 = {}
        pe_pre_f1 = sPE.n
        dv_pre_f1 = sDV.n
        hid1v = hid1d.rearrange("(p2 j g) t -> p2 j g t", p2=64, j=2, g=64)

        def f1_back(idx):
            pp = idx % 2
            hg, th = divmod(idx, 2)

            @block.tensor
            def _(q, pp=pp, idx=idx):
                q.wait_ge(sDV.h, f1[(idx, "mult")])
                if idx - 2 >= 0:
                    q.wait_ge(sDV.h, f1[(idx - 2, "h1ev")])
                ins = q.matmul(ps_h1[pp][:], h128_s[:], hidb[pp][:], start=True, stop=True)
                sPE.inc(ins)
                f1[(idx, "h128")] = sPE.n

            @block.vector
            def _(q, pp=pp, idx=idx):
                q.wait_ge(sPE.h, f1[(idx, "h128")])
                if idx - 2 >= 0:
                    wtok(q, f1[(idx - 2, "h1st")])
                ins = q.tensor_copy(out=h1b[pp][:], in_=ps_h1[pp][:])
                sDV.inc(ins)
                f1[(idx, "h1ev")] = sDV.n

            @block.sync
            def _(q, hg=hg, th=th, pp=pp, idx=idx):
                q.wait_ge(sDV.h, f1[(idx, "h1ev")])
                ins = q.dma_start(out=hid1v[:, :, hg, th * THF:(th + 1) * THF],
                                  in_=h1b[pp][:])
                f1[(idx, "h1st")] = h1S.put(ins)

        for hg in range(HG):
            @block.gpsimd
            def _(q, hg=hg):
                if hg < 2:
                    q.wait_ge(sPE.h, pe_pre_f1)
                    wtok(q, attn_parked)
                    q.wait_ge(sDV.h, dv_pre_f1)
                else:
                    q.wait_ge(sPE.h, f1[(2 * (hg - 2) + 1, "mm")])
                ia = q.dma_start(out=w1buf[hg % 2][:], in_=w1_st[hg])
                ib = q.dma_start(out=w3buf[hg % 2][:], in_=w3_st[hg])
                w13_ready[hg] = w13S.put2(ia, ib)

            for th in range(2):
                idx = 2 * hg + th
                pp = idx % 2
                @block.tensor
                def _(q, hg=hg, th=th, pp=pp, idx=idx):
                    if idx == 0:
                        q.wait_ge(sAC.h, sAC.n)
                        q.wait_ge(sDV.h, sDV.n)
                    wtok(q, w13_ready[hg])
                    if idx - 2 >= 0:
                        q.wait_ge(sAC.h, f1[(idx - 2, "sil")])
                        q.wait_ge(sDV.h, f1[(idx - 2, "mult")])
                    for g in range(G):
                        q.matmul(ps_g[pp][:], w1buf[hg % 2][:, g * 128:(g + 1) * 128],
                                 xn[:, g * T + th * THF: g * T + (th + 1) * THF],
                                 start=(g == 0), stop=(g == G - 1))
                    for g in range(G):
                        ins = q.matmul(ps_l[pp][:], w3buf[hg % 2][:, g * 128:(g + 1) * 128],
                                       xn[:, g * T + th * THF: g * T + (th + 1) * THF],
                                       start=(g == 0), stop=(g == G - 1))
                    sPE.inc(ins)
                    f1[(idx, "mm")] = sPE.n

                @block.scalar
                def _(q, pp=pp, idx=idx):
                    q.wait_ge(sPE.h, f1[(idx, "mm")])
                    if idx - 2 >= 0:
                        q.wait_ge(sDV.h, f1[(idx - 2, "mult")])
                    ins = q.activation(out=silb[pp][:], in_=ps_g[pp][:], func=AF.Silu)
                    sAC.inc(ins)
                    f1[(idx, "sil")] = sAC.n

                @block.vector
                def _(q, pp=pp, idx=idx):
                    q.wait_ge(sAC.h, f1[(idx, "sil")])
                    if idx - 2 >= 0:
                        q.wait_ge(sPE.h, f1[(idx - 2, "h128")])
                    ins = q.tensor_tensor(out=hidb[pp][:], in0=silb[pp][:],
                                          in1=ps_l[pp][:], op=OP.mult)
                    sDV.inc(ins)
                    f1[(idx, "mult")] = sDV.n

                if idx - 1 >= 0:
                    f1_back(idx - 1)
        f1_back(2 * HG - 1)
        hid1_done = h1S.totals()
        ps_f1s.close()
        es_f1.close()

        if "hid1" in taps:
            tp = mktap("hid1", T, BF16, rows=HG * 128)
            @block.sync
            def _(q, tp=tp):
                wtok(q, hid1_done)
                tapS.put(q.dma_start(out=tp, in_=hid1d))

        if stop_after == "f1":
            finish_early()
            return nc

        es_f2 = contextlib.ExitStack()
        ACC = es_f2.enter_context(sb("ACC", [128, G * T], F32))
        hid2 = es_f2.enter_context(sb("hid2", [128, U * THF], BF16))
        hp2 = [es_f2.enter_context(sb(f"hp2_{i}", [128, 4 * THF], BF16)) for i in range(2)]
        w2buf = [es_f2.enter_context(sb(f"w2buf{i}", [128, 8 * 128], BF16)) for i in range(6)]

        hview = hid1d.rearrange("(u r) t -> r u t", u=U)
        f2 = {}
        w2_ready = {}

        es_out = contextlib.ExitStack()
        x2b = [es_out.enter_context(sb(f"x2b{i}", [128, THF], F32)) for i in range(2)]
        atb = [es_out.enter_context(sb(f"atb{i}", [128, THF], F32)) for i in range(2)]
        at2b = [es_out.enter_context(sb(f"at2b{i}", [128, THF], F32)) for i in range(2)]
        stg = [es_out.enter_context(sb(f"stg{i}", [128, THF], F32)) for i in range(2)]
        oev = {}
        o_i = [0]

        def emit_out(j, acc_ready):
            oi = o_i[0]
            o_i[0] += 1
            cs = slice(j * THF, (j + 1) * THF)
            g, hf = divmod(j, 2)

            @block.scalar
            def _(q, j=j, oi=oi, cs=cs, g=g, hf=hf):
                if oi == 0:
                    wtok(q, attn_parked)
                if (oi - 2, "st") in oev:
                    wtok(q, oev[(oi - 2, "st")])
                ia = q.dma_start(out=x2b[oi % 2][:], in_=x_st[:, cs])
                ib = q.dma_start(out=atb[oi % 2][:],
                                 in_=attnd[0][g * 128:(g + 1) * 128, hf * THF:(hf + 1) * THF])
                ic = q.dma_start(out=at2b[oi % 2][:],
                                 in_=attnd[1][g * 128:(g + 1) * 128, hf * THF:(hf + 1) * THF])
                s_ = outldS.s[outldS.i % outldS.k]
                s_.inc(ia, 16)
                s_.inc(ib, 16)
                s_.inc(ic, 16)
                outldS.i += 1
                oev[(oi, "ld")] = (s_, s_.n)

            @block.vector
            def _(q, j=j, oi=oi, cs=cs, acc_ready=acc_ready):
                wtok(q, oev[(oi, "ld")])
                q.wait_ge(sDV.h, acc_ready)
                q.wait_ge(sDV.h, sDV.n)
                sDV.inc(q.tensor_tensor(out=stg[oi % 2][:], in0=ACC[:, cs],
                                        in1=x2b[oi % 2][:], op=OP.add))
                q.wait_ge(sDV.h, sDV.n)
                sDV.inc(q.tensor_tensor(out=x2b[oi % 2][:], in0=stg[oi % 2][:],
                                        in1=atb[oi % 2][:], op=OP.add))
                q.wait_ge(sDV.h, sDV.n)
                ins = q.tensor_tensor(out=stg[oi % 2][:], in0=x2b[oi % 2][:],
                                      in1=at2b[oi % 2][:], op=OP.add)
                sDV.inc(ins)
                oev[(oi, "sum")] = sDV.n

            @block.scalar
            def _(q, oi=oi, cs=cs):
                q.wait_ge(sDV.h, oev[(oi, "sum")])
                ins = q.dma_start(out=out_st[:, cs], in_=stg[oi % 2][:])
                oev[(oi, "st")] = outstS.put(ins)

        for th in range(2):
            bidx = th
            ps_h2x = contextlib.ExitStack()
            ps_h2 = [ps_h2x.enter_context(nc.psum_tensor(f"ps_h2_{th}_{i}", [128, THF], F32))
                     for i in range(4)]
            for ub in range(U // 4):
                @block.sync
                def _(q, bidx=bidx, th=th, ub=ub):
                    if bidx == 0 and ub == 0:
                        wtok(q, hid1_done)
                    pk = (bidx, 4 * (ub - 2) + 3, "h64") if ub >= 2 else (bidx - 1, 4 * (ub + 14) + 3, "h64")
                    if pk in f2:
                        q.wait_ge(sPE.h, f2[pk])
                    ins = q.dma_start(out=hp2[ub % 2][:],
                                      in_=hview[:, ub * 4:(ub + 1) * 4, th * THF:(th + 1) * THF])
                    f2[(bidx, ub, "hp")] = hpS.put(ins)

                for k in range(4):
                    u = ub * 4 + k
                    @block.tensor
                    def _(q, bidx=bidx, ub=ub, k=k, u=u):
                        if u == 0:
                            if bidx == 0:
                                q.wait_ge(sAC.h, sAC.n)
                                q.wait_ge(sDV.h, sDV.n)
                            else:
                                q.wait_ge(sPE.h, f2[(1, "w2mm")])
                                q.wait_ge(sDV.h, f2[(1, "accev")])
                        if k == 0:
                            wtok(q, f2[(bidx, ub, "hp")])
                        if (bidx, u - 4, "h2ev") in f2:
                            ev = f2[(bidx, u - 4, "h2ev")]
                            q.wait_ge(ev[0].h, ev[1])
                        ins = q.matmul(ps_h2[u % 4][:], hbd_s[:],
                                       hp2[ub % 2][:, k * THF:(k + 1) * THF],
                                       start=True, stop=True)
                        sPE.inc(ins)
                        f2[(bidx, u, "h64")] = sPE.n

                    if u % 2 == 0:
                        @block.vector
                        def _(q, bidx=bidx, u=u):
                            q.wait_ge(sPE.h, f2[(bidx, u, "h64")])
                            ins = q.tensor_scalar(out=hid2[:, u * THF:(u + 1) * THF],
                                                  in0=ps_h2[u % 4][:], scalar1=FSCALE,
                                                  scalar2=None, op0=OP.mult)
                            sDV.inc(ins)
                            f2[(bidx, u, "h2ev")] = (sDV, sDV.n)
                    else:
                        @block.scalar
                        def _(q, bidx=bidx, u=u):
                            q.wait_ge(sPE.h, f2[(bidx, u, "h64")])
                            ins = q.activation(out=hid2[:, u * THF:(u + 1) * THF],
                                               in_=ps_h2[u % 4][:], func=AF.Copy,
                                               scale=FSCALE)
                            sAC.inc(ins)
                            f2[(bidx, u, "h2ev")] = (sAC, sAC.n)
            hid2_done = (sDV.n, sAC.n)
            ps_h2x.close()

            for dh in range(2):
                wsub = th * 2 + dh
                ps_w2x = contextlib.ExitStack()
                ps_w2 = ps_w2x.enter_context(nc.psum_tensor(f"ps_w2_{wsub}", [128, 8 * THF], F32))
                for u in range(U):
                    idx = wsub * U + u
                    @block.gpsimd
                    def _(q, u=u, idx=idx, dh=dh):
                        if idx < 6:
                            wtok(q, hid1_done)
                        else:
                            q.wait_ge(sPE.h, f2[(idx - 6, "w2u")])
                        w2_ready[idx] = w2S.put(
                            q.dma_start(out=w2buf[idx % 6][:],
                                        in_=w2_st[u][:, dh * 1024:(dh + 1) * 1024]))

                    @block.tensor
                    def _(q, wsub=wsub, u=u, idx=idx, hid2_done=hid2_done):
                        if u == 0:
                            q.wait_ge(sDV.h, hid2_done[0])
                            q.wait_ge(sAC.h, hid2_done[1])
                            if (wsub - 1, "accev") in f2:
                                q.wait_ge(sDV.h, f2[(wsub - 1, "accev")])
                        wtok(q, w2_ready[idx])
                        for dtile in range(8):
                            ins = q.matmul(ps_w2[:, dtile * THF:(dtile + 1) * THF],
                                           w2buf[idx % 6][:, dtile * 128:(dtile + 1) * 128],
                                           hid2[:, u * THF:(u + 1) * THF],
                                           start=(u == 0), stop=(u == U - 1))
                        sPE.inc(ins)
                        f2[(idx, "w2u")] = sPE.n
                f2[(wsub, "w2mm")] = sPE.n

                @block.vector
                def _(q, dh=dh, th=th, wsub=wsub):
                    q.wait_ge(sPE.h, f2[(wsub, "w2mm")])
                    for dtile in range(8):
                        jt = dh * 8 + dtile
                        ins = q.tensor_copy(
                            out=ACC[:, jt * T + th * THF: jt * T + (th + 1) * THF],
                            in_=ps_w2[:, dtile * THF:(dtile + 1) * THF])
                    sDV.inc(ins)
                    f2[(wsub, "accev")] = sDV.n
                ps_w2x.close()
                for dtile in range(8):
                    emit_out(2 * (dh * 8 + dtile) + th, f2[(wsub, "accev")])
        acc_done = sDV.n

        if "acc" in taps:
            tp = mktap("acc", G * T, F32)
            @block.sync
            def _(q, tp=tp):
                q.wait_ge(sDV.h, acc_done)
                tapS.put(q.dma_start(out=tp, in_=ACC[:]))

        @block.sync
        def _(q):
            q.wait_ge(sSTO.h, sSTO.n)
            for strm in (parkS, h1S, outstS, tapS):
                wtok(q, strm.totals())

        es_out.close()
        es_f2.close()

    return nc


def _host_layouts(inputs):
    f32 = lambda a: np.ascontiguousarray(np.asarray(a, dtype=np.float32))
    x = f32(inputs["L0_attn_x"])[0]
    wqT = f32(inputs["L0_attn_wq"]).T
    wkT = f32(inputs["L0_attn_wk"]).T
    wvT = f32(inputs["L0_attn_wv"]).T
    woT = f32(inputs["L0_attn_wo"]).T
    w1T = f32(inputs["L0_ffn_W1"]).T
    w3T = f32(inputs["L0_ffn_W3"]).T
    W2T = f32(inputs["L0_ffn_W2"]).T

    def stream_lhsT(wT):
        n = wT.shape[1] // 128
        return np.ascontiguousarray(
            wT.reshape(G, 128, n, 128).transpose(2, 1, 0, 3).reshape(n, 128, G * 128))

    consts = dict(
        identc=np.eye(128, dtype=ml_dtypes.bfloat16),
        h128c=_hadamard(128).astype(ml_dtypes.bfloat16),
        hbdc=np.kron(np.eye(2), _hadamard(64)).astype(ml_dtypes.bfloat16),
        maskc=np.ascontiguousarray(
            np.tile(np.triu(np.ones((64, 64), np.float32)), (1, 4))),
        onesbc=np.ones((128, 1), ml_dtypes.bfloat16),
        onespc=np.ones((1, 128), np.float32),
        onesnc=-np.ones((1, 128), np.float32),
    )
    col = lambda v: np.ascontiguousarray(f32(v).reshape(128, 1))
    base = dict(
        wq_st=stream_lhsT(wqT), wk_st=stream_lhsT(wkT),
        wv_st=np.ascontiguousarray(
            wvT.reshape(G, 128, 4, THF).transpose(2, 1, 0, 3).reshape(4, 128, G * THF)),
        wo_st=np.ascontiguousarray(
            woT.reshape(2, 8, 128, G, 128).transpose(0, 3, 2, 1, 4).reshape(2, G, 128, 8 * 128)),
        w1_st=stream_lhsT(w1T), w3_st=stream_lhsT(w3T),
        w2_st=np.ascontiguousarray(
            W2T.reshape(64, 64, 2, D).transpose(1, 2, 0, 3).reshape(U, 128, D)),
        fq1T=np.ascontiguousarray(f32(inputs["L0_attn_fq1_w"]).T),
        fq2T=np.ascontiguousarray(f32(inputs["L0_attn_fq2_w"]).T),
        fk1T=np.ascontiguousarray(f32(inputs["L0_attn_fk1_w"]).T),
        fk2T=np.ascontiguousarray(f32(inputs["L0_attn_fk2_w"]).T),
        fq1b=col(inputs["L0_attn_fq1_b"]), fq2b=col(inputs["L0_attn_fq2_b"]),
        fk1b=col(inputs["L0_attn_fk1_b"]), fk2b=col(inputs["L0_attn_fk2_b"]),
        normw=col(inputs["L0_attn_norm_w"]),
        **consts,
    )
    in_maps = []
    for c in range(NCORES):
        xc = np.ascontiguousarray(
            x[c * T:(c + 1) * T, :].T.reshape(G, 128, T).transpose(1, 0, 2).reshape(128, G * T))
        w = np.zeros((128, NCORES), np.float32)
        w[:, :c] = 1.0
        in_maps.append(dict(base, x_st=xc, wsel=w))
    return in_maps


def kernel(**inputs):
    in_maps = _host_layouts(inputs)
    nc = build()
    res = run_bass_kernel_spmd(nc, in_maps, core_ids=list(range(NCORES)))
    outs = []
    for c in range(NCORES):
        o = res.results[c]["out_st"]
        o = o.reshape(128, G, T).transpose(1, 0, 2).reshape(D, T).T
        outs.append(o)
    return np.concatenate(outs, axis=0)[None, :, :].astype(np.float32)
